# version 07
# baseline (speedup 1.0000x reference)
import contextlib
import numpy as np
import ml_dtypes
import concourse.bass as bass
import concourse.mybir as mybir
from concourse.bass_utils import run_bass_kernel_spmd

F32 = mybir.dt.float32
F32R = mybir.dt.float32
BF16 = mybir.dt.bfloat16
AF = mybir.ActivationFunctionType
ALU = mybir.AluOpType

BSH, T, C, H, D = 2, 4096, 1024, 16, 64
NBLK, BL = 16, 256
EXPHALF = float(np.exp(-0.5))
ENG_NAME = {"t": "tensor", "v": "vector", "s": "scalar", "g": "gpsimd", "y": "sync"}


class Em:
    def __init__(self, nc, block, ctx, serial=False):
        self.nc, self.block = nc, block
        self.serial = serial
        self.sems = {}
        self.cnt = {e: 0 for e in ENG_NAME}
        for e in ENG_NAME:
            self.sems[e] = ctx.enter_context(nc.semaphore(f"sem_{e}"))

    def emit(self, eng, fn, deps, dma=False):
        merged = {}
        if self.serial:
            deps = list(deps) + [(e, self.cnt[e]) for e in ENG_NAME]
        for d in deps:
            if d is None:
                continue
            e, c = d
            if c > 0:
                merged[e] = max(merged.get(e, 0), c)
        sems = self.sems
        dec = getattr(self.block, ENG_NAME[eng])
        ndma = [0]

        class _Px:
            def __init__(px, h): px.h = h
            def dma_start(px, **kw):
                inst = px.h.dma_start(**kw)
                inst.then_inc(sems[eng], 16)
                ndma[0] += 1
                return inst
            def __getattr__(px, a): return getattr(px.h, a)

        @dec
        def _(h):
            for e, c in sorted(merged.items()):
                h.wait_ge(sems[e], c)
            if dma:
                fn(_Px(h))
            else:
                last = fn(h)
                assert last is not None
                last.then_inc(sems[eng], 1)
        self.cnt[eng] += 16 * ndma[0] if dma else 1
        return (eng, self.cnt[eng])

    def t(self, fn, *deps): return self.emit("t", fn, deps)
    def v(self, fn, *deps): return self.emit("v", fn, deps)
    def s(self, fn, *deps): return self.emit("s", fn, deps)
    def g(self, fn, *deps): return self.emit("g", fn, deps, dma=True)
    def y(self, fn, *deps): return self.emit("y", fn, deps, dma=True)

    def barrier(self, eng, deps):
        merged = {}
        for d in deps:
            if d is None:
                continue
            e, c = d
            merged[e] = max(merged.get(e, 0), c)
        sems = self.sems
        dec = getattr(self.block, ENG_NAME[eng])

        @dec
        def _(h):
            for e, c in sorted(merged.items()):
                h.wait_ge(sems[e], c)


def build_pc(dbg=False, serial=True):
    ctx = contextlib.ExitStack()
    nc = bass.Bass("TRN2", target_bir_lowering=False)
    ap = lambda name, shp, dt, k="ExternalInput": nc.dram_tensor(name, shp, dt, kind=k).ap()
    x1b = ap("x1b", [BSH, T, C], F32)
    z1tb = ap("z1tb", [BSH, C, T + 1], F32)
    x2b = ap("x2b", [BSH, T, C], F32, "ExternalOutput")
    dbg_out = {}
    if dbg:
        for nm in ["d_mixr", "d_rf", "d_kk", "d_vf", "d_wbc", "d_kv", "d_st", "d_xng", "d_sm"]:
            dbg_out[nm] = ap(nm, [128, 8 * BL], F32, "ExternalOutput")
    Wi = {n: ap(n, [C, C], F32R) for n in ["Wr_p", "Wk_p", "Wv_p", "Wo_p"]}
    w1i, a1i, g1i = ap("w1", [C, 64], F32R), ap("a1", [C, 64], F32R), ap("g1", [C, 64], F32R)
    w2i, a2i = ap("w2", [64, 16], F32R), ap("a2", [64, 16], F32R)
    g2i = ap("g2_p", [64, C], F32R)
    musi = ap("mus", [128, 48], F32)
    kkfi = ap("kkf", [128, 8], F32)
    w0i, a0i = ap("w0h", [16, 1], F32), ap("a0h", [16, 1], F32)
    E16i = ap("E16p", [16, 128], F32R)
    E16fi = ap("E16f", [16, 128], F32)
    seli = ap("sel16", [128, 16], F32R)
    rkei = ap("rke", [128, 8 * 16], F32R)
    lnwi = ap("lnwf", [128, 8], F32)
    lnbi = ap("lnbf", [128, 8], F32)

    sbuf = lambda name, shp, dt: ctx.enter_context(nc.sbuf_tensor(name, shp, dt))
    psum = lambda name, shp: ctx.enter_context(nc.psum_tensor(name, shp, F32))

    Wsb = {n: sbuf(f"W{n}_sb", [128, 8 * 1024], F32R) for n in ["r", "k", "v", "o"]}
    l1_sb = sbuf("l1_sb", [128, 3 * 512], F32R)
    l2_sb = sbuf("l2_sb", [64, 32 + 1024], F32R)
    mus_sb = sbuf("mus_sb", [128, 48], F32)
    kkf_sb = sbuf("kkf_sb", [128, 8], F32)
    w0_sb = sbuf("w0_sb", [16, 1], F32)
    a0_sb = sbuf("a0_sb", [16, 1], F32)
    E16_sb = sbuf("E16_sb", [16, 128], F32R)
    E16f_sb = sbuf("E16f_sb", [16, 128], F32)
    sel_sb = sbuf("sel_sb", [128, 16], F32R)
    rke_sb = sbuf("rke_sb", [128, 8 * 16], F32R)
    lnw_sb = sbuf("lnw_sb", [128, 8], F32)
    lnb_sb = sbuf("lnb_sb", [128, 8], F32)
    zt = sbuf("zt", [128, 8 * (BL + 1)], F32)
    sx = sbuf("sx", [128, 8 * BL], F32)
    mixb = sbuf("mixb", [128, 8 * BL], F32R)
    r_f = sbuf("r_f", [128, 8 * BL], F32)
    kk_f = sbuf("kk_f", [128, 8 * BL], F32R)
    v_f = sbuf("v_f", [128, 8 * BL], F32)
    st_f = sbuf("st_f", [128, 8 * BL], F32R)
    wbc = sbuf("wbc", [128, BL], F32)
    sq_f = sbuf("sq_f", [128, BL], F32R)
    g_scr = sbuf("g_scr", [128, BL], F32)
    tl = sbuf("tl", [64, BL], F32R)
    xng = mixb
    carry = sbuf("carry", [128, 8], F32)
    sm0 = sbuf("sm0", [16, BL], F32)
    sm1 = sbuf("sm1", [16, BL], F32)
    smI = sbuf("smI", [16, BL], F32R)
    smW = sbuf("smW", [16, BL], F32)
    smB = sbuf("smB", [16, BL], F32R)
    smC = sbuf("smC", [16, BL], F32R)
    eps16 = sbuf("eps16", [16, 1], F32)
    x1t = g_scr

    pj = [psum("pj0", [128, 512]), psum("pj1", [128, 512])]
    pb = psum("pb", [128, 512])
    PX = psum("PX", [128, 512])
    PSA = psum("PSA", [16, 512])
    pstB = psum("pstB", [16, 512])
    pd = psum("pd", [128, 1024])

    pl64 = PX[0:64, 0:BL]
    pls16 = PSA[0:16, 0:BL]

    with ctx:
        block = ctx.enter_context(nc.Block())
        em = Em(nc, block, ctx, serial=serial)

        def wload(g):
            last = None
            for k in range(8):
                for n, src in [("r", Wi["Wr_p"]), ("k", Wi["Wk_p"]), ("v", Wi["Wv_p"]), ("o", Wi["Wo_p"])]:
                    last = g.dma_start(out=Wsb[n][:, k * 1024:(k + 1) * 1024],
                                       in_=src[k * 128:(k + 1) * 128, :])
                last = g.dma_start(out=l1_sb[:, k * 64:(k + 1) * 64], in_=w1i[k * 128:(k + 1) * 128, :])
                last = g.dma_start(out=l1_sb[:, 512 + k * 64:512 + (k + 1) * 64], in_=a1i[k * 128:(k + 1) * 128, :])
                last = g.dma_start(out=l1_sb[:, 1024 + k * 64:1024 + (k + 1) * 64], in_=g1i[k * 128:(k + 1) * 128, :])
            last = g.dma_start(out=l2_sb[:, 0:16], in_=w2i)
            last = g.dma_start(out=l2_sb[:, 16:32], in_=a2i)
            last = g.dma_start(out=l2_sb[:, 32:32 + 1024], in_=g2i)
            last = g.dma_start(out=mus_sb[:], in_=musi)
            last = g.dma_start(out=kkf_sb[:], in_=kkfi)
            last = g.dma_start(out=w0_sb[:], in_=w0i)
            last = g.dma_start(out=a0_sb[:], in_=a0i)
            last = g.dma_start(out=E16_sb[:], in_=E16i)
            last = g.dma_start(out=E16f_sb[:], in_=E16fi)
            last = g.dma_start(out=sel_sb[:], in_=seli)
            last = g.dma_start(out=rke_sb[:], in_=rkei)
            last = g.dma_start(out=lnw_sb[:], in_=lnwi)
            last = g.dma_start(out=lnb_sb[:], in_=lnbi)
            return last
        WL = em.g(wload)
        EPS = em.v(lambda v: v.memset(eps16[:], 1e-5))

        chk = lambda buf, c, n=BL: buf[:, c * n:(c + 1) * n]

        def dump(nm, src_ap, *deps):
            if not dbg or nm not in dbg_out:
                return None
            n = src_ap.shape[-1]
            return em.g(lambda g: g.dma_start(out=dbg_out[nm][:, 0:n], in_=src_ap), *deps)

        def dump_sm(col, src_ap, *deps):
            if not dbg:
                return None
            return em.g(lambda g: g.dma_start(out=dbg_out["d_sm"][0:16, col * BL:(col + 1) * BL],
                                              in_=src_ap), *deps)

        tk = {
            "mix_free": None, "sx_free": None, "zt_free": None,
            "evac": [None, None],
            "pb_free": None, "PX_free": None, "PSA_free": None, "pstB_free": None,
            "pd_free": None, "x1t_free": None, "sm_free": None,
            "kkv_free": None, "smW_free": None, "smC_free": None, "smR_free": None, "smB_free": None, "g_free": None,
            "store": None, "dbg": None,
        }

        for p in range(BSH):
            CR = em.v(lambda v: v.memset(carry[:], 0.0), tk["kkv_free"])
            for j in range(NBLK):
                t0 = j * BL

                def zload(yq, p=p, t0=t0):
                    last = None
                    for c in range(8):
                        last = yq.dma_start(out=chk(zt, c, BL + 1),
                                            in_=z1tb[p, c * 128:(c + 1) * 128, t0:t0 + BL + 1])
                    return last
                ZL = em.y(zload, tk["zt_free"])

                def mk_sx(v):
                    last = None
                    for c in range(8):
                        zc = chk(zt, c, BL + 1)
                        last = v.tensor_tensor(chk(sx, c), zc[:, 1:1 + BL], zc[:, 0:BL], ALU.subtract)
                    return last
                SX = em.v(mk_sx, ZL, tk["sx_free"], tk["dbg"])

                def mix(name, *deps):
                    mcol = {"r": 0, "w": 1, "k": 2, "v": 3, "a": 4, "g": 5}[name]

                    def f(v, mcol=mcol):
                        last = None
                        for c in range(8):
                            zc = chk(zt, c, BL + 1)
                            last = v.scalar_tensor_tensor(
                                chk(mixb, c), chk(sx, c),
                                mus_sb[:, mcol * 8 + c:mcol * 8 + c + 1],
                                zc[:, 1:1 + BL], ALU.mult, ALU.add)
                        return last
                    return em.v(f, SX, WL, *deps)

                def proj(wname, evac_fn, MIX, extra=()):
                    MM = None
                    for m in range(8):
                        def mm(t, m=m, wname=wname):
                            last = None
                            for k in range(8):
                                last = t.matmul(pj[m % 2][:, 0:BL],
                                                Wsb[wname][:, k * 1024 + m * 128:k * 1024 + m * 128 + 128],
                                                chk(mixb, k), start=(k == 0), stop=(k == 7))
                            return last
                        MM = em.t(mm, MIX, tk["evac"][m % 2], *extra)
                        ed = (tk["kkv_free"],) if m == 0 else ()
                        E = em.s(lambda s, m=m: evac_fn(s, m), MM, *ed)
                        tk["evac"][m % 2] = E
                    return MM

                MIXR = mix("r", tk["mix_free"], tk["kkv_free"])
                MMR = proj("r", lambda s, m: s.copy(chk(r_f, m), pj[m % 2][:, 0:BL]), MIXR)
                RL = tk["evac"][1]

                MIXK = mix("k", MMR)
                MMK = proj("k", lambda s, m: s.activation(
                    chk(kk_f, m), pj[m % 2][:, 0:BL], AF.Copy,
                    scale=kkf_sb[:, m:m + 1]), MIXK)

                N2 = None
                SQ = None
                for c in range(8):
                    SQ = em.s(lambda s, c=c: s.activation(sq_f[:],
                                                          chk(kk_f, c).bitcast(F32), AF.Square),
                              tk["evac"][0], tk["evac"][1], N2)
                    N2 = em.t(lambda t, c=c: t.matmul(pls16, sel_sb[:], sq_f[:],
                                                      start=(c == 0), stop=(c == 7)),
                              SQ, WL, tk["PSA_free"])
                MX = em.v(lambda v: v.tensor_scalar_max(sm0[:], pls16, 1e-24), N2, tk["sm_free"])
                LNN = em.s(lambda s: s.activation(sm1[:], sm0[:], AF.Ln), MX)
                INV = em.s(lambda s: s.activation(smI[:], sm1[:], AF.Exp, scale=-0.5), LNN)

                MIXV = mix("v", MMK)
                MMV = proj("v", lambda s, m: s.copy(chk(v_f, m), pj[m % 2][:, 0:BL]), MIXV)
                EV_ALL = (tk["evac"][0], tk["evac"][1])

                def mml(t, off):
                    last = None
                    for k in range(8):
                        last = t.matmul(pl64, l1_sb[:, off + k * 64:off + (k + 1) * 64],
                                        chk(mixb, k), start=(k == 0), stop=(k == 7))
                    return last
                MIXW = mix("w", MMV)
                W1 = em.t(lambda t: mml(t, 0), MIXW, tk["PX_free"])
                TW = em.s(lambda s: s.activation(tl[:], pl64, AF.Tanh), W1)
                W2 = em.t(lambda t: t.matmul(pls16, l2_sb[:, 0:16], tl[:], start=True, stop=True),
                          TW, MX)
                WSG = em.s(lambda s: s.activation(sm1[:], pls16, AF.Sigmoid, bias=w0_sb[:]), W2, INV)
                WSM = em.v(lambda v: v.tensor_scalar_mul(smW[:], sm1[:], EXPHALF), WSG, tk["smW_free"])

                MIXA = mix("a", W1)
                A1 = em.t(lambda t: mml(t, 512), MIXA, TW)
                TA = em.s(lambda s: s.activation(tl[:], pl64, AF.Tanh), A1, W2)
                A2 = em.t(lambda t: t.matmul(pls16, l2_sb[:, 16:32], tl[:], start=True, stop=True),
                          TA, WSG)
                ASG = em.s(lambda s: s.activation(sm1[:], pls16, AF.Sigmoid, bias=a0_sb[:]), A2, WSM)
                CSM = em.v(lambda v: v.tensor_tensor(smC[:], sm1[:], smI[:].bitcast(F32), ALU.mult),
                           ASG, INV, tk["smC_free"])

                WBC = em.t(lambda t: t.matmul(pb[:, 0:BL], E16f_sb[:], smW[:], start=True, stop=True),
                           WSM, WL, tk["pb_free"])
                WBE = em.s(lambda s: s.copy(wbc[:], pb[:, 0:BL]), WBC)
                D_WBC = dump("d_wbc", wbc[:], WBE) if (p == 0 and j == 0) else None
                CBC = em.t(lambda t: t.matmul(PX[:, 0:BL], E16_sb[:], smC[:], start=True, stop=True),
                           CSM, TA)

                def mk_kv1(v):
                    last = None
                    for c in range(8):
                        last = v.tensor_tensor(chk(st_f, c), chk(kk_f, c).bitcast(F32),
                                               chk(v_f, c), ALU.mult)
                    return last
                KV1 = em.v(mk_kv1, MMV, tk["evac"][0], tk["evac"][1], CSM)

                def mk_kv2(v):
                    last = None
                    for c in range(8):
                        last = v.tensor_tensor(chk(st_f, c), chk(st_f, c).bitcast(F32),
                                               PX[:, 0:BL], ALU.mult)
                    return last
                KV2 = em.v(mk_kv2, KV1, CBC)
                D_KV = dump("d_kv", st_f[:].bitcast(F32), KV2) if (p == 0 and j == 0) else None

                def mk_scan(v):
                    last = None
                    for c in range(8):
                        last = v.tensor_tensor_scan(chk(st_f, c), wbc[:], chk(st_f, c).bitcast(F32),
                                                    carry[:, c:c + 1], ALU.mult, ALU.add)
                    return last
                SCN = em.v(mk_scan, KV2, WBE, CR, D_KV)

                def mk_carry(v):
                    last = None
                    for c in range(8):
                        last = v.tensor_copy(carry[:, c:c + 1], chk(st_f, c).bitcast(F32)[:, BL - 1:BL])
                    return last
                CARRY = em.v(mk_carry, SCN)
                D_ST = dump("d_st", st_f[:].bitcast(F32), CARRY) if (p == 0 and j == 0) else None

                INMM = None
                RK1 = None
                for c in range(8):
                    RK1 = em.v(lambda v, c=c: v.tensor_tensor(sq_f[:], chk(r_f, c),
                                                              chk(kk_f, c).bitcast(F32), ALU.mult),
                               EV_ALL[0], EV_ALL[1], INMM, N2, KV2)
                    INMM = em.t(lambda t, c=c: t.matmul(pls16, rke_sb[:, c * 16:(c + 1) * 16],
                                                        sq_f[:], start=(c == 0), stop=(c == 7)),
                                RK1, ASG if c == 0 else None)
                INEV = em.v(lambda v: v.tensor_tensor(smC[:], pls16, smI[:].bitcast(F32), ALU.mult),
                            INMM, CBC, CSM)
                D_IN = dump_sm(4, smC[:].bitcast(F32), INEV) if (p == 0 and j == 0) else None
                BCMM = em.t(lambda t: t.matmul(pb[:, 0:BL], E16_sb[:], smC[:], start=True, stop=True),
                            INEV, WBE)
                def mk_xw1(v):
                    last = None
                    for c in range(8):
                        last = v.tensor_tensor(chk(st_f, c), chk(st_f, c).bitcast(F32),
                                               chk(r_f, c), ALU.mult)
                    return last
                XW1 = em.v(mk_xw1, SCN, CARRY, D_KV, D_ST)
                XW2 = None
                for c in range(8):
                    BONC = em.v(lambda v, c=c: v.tensor_tensor(wbc[:], chk(v_f, c), pb[:, 0:BL], ALU.mult),
                                BCMM, XW1 if c == 0 else XW2, SCN if c == 0 else None, D_WBC)
                    XW2 = em.v(lambda v, c=c: v.tensor_tensor(chk(st_f, c),
                                                              chk(st_f, c).bitcast(F32), wbc[:], ALU.add),
                               BONC)

                SMM = None
                for c in range(8):
                    SMM = em.t(lambda t, c=c: t.matmul(pls16, sel_sb[:], chk(st_f, c),
                                                       start=(c == 0), stop=(c == 7)),
                               XW2, INEV, SMM)
                SQMM = None
                SQS = None
                for c in range(8):
                    SQS = em.s(lambda s, c=c: s.activation(sq_f[:],
                                                           chk(st_f, c).bitcast(F32), AF.Square),
                               XW2, SQMM, BCMM)
                    SQMM = em.t(lambda t, c=c: t.matmul(pstB[:, 0:BL], sel_sb[:], sq_f[:],
                                                        start=(c == 0), stop=(c == 7)),
                                SQS, tk["pstB_free"])
                MU = em.v(lambda v: v.tensor_scalar_mul(sm0[:], pls16, 1.0 / 64), SMM, LNN)
                M2 = em.v(lambda v: v.tensor_tensor(sm1[:], sm0[:], sm0[:], ALU.mult), MU, INV)
                VAR = em.v(lambda v: v.scalar_tensor_tensor(sm1[:], pstB[:, 0:BL], 1.0 / 64, sm1[:],
                                                            ALU.mult, ALU.subtract), M2, SQMM)
                LNV = em.s(lambda s: s.activation(smI[:], sm1[:], AF.Ln, bias=eps16[:]), VAR, BCMM, EPS, D_IN)
                RSTD = em.s(lambda s: s.activation(smC[:], smI[:].bitcast(F32), AF.Exp, scale=-0.5), LNV, tk["smR_free"], BCMM)
                B0 = em.v(lambda v: v.tensor_tensor(sm1[:], sm0[:], smC[:].bitcast(F32), ALU.mult),
                          RSTD, VAR)
                BN = em.v(lambda v: v.tensor_scalar_mul(smB[:], sm1[:], -1.0), B0, tk["smB_free"])

                MIXG = mix("g", A1)
                G1 = em.t(lambda t: mml(t, 1024), MIXG, TA, CBC, KV2)
                TG = em.s(lambda s: s.activation(tl[:], pl64, AF.Tanh), G1, A2)

                XNGDEP = []
                UT = None
                AMM = em.t(lambda t: t.matmul(pb[:, 0:BL], E16_sb[:], smC[:], start=True, stop=True),
                           RSTD, BCMM, XW2)
                BMM = em.t(lambda t: t.matmul(PX[:, 0:BL], E16_sb[:], smB[:], start=True, stop=True),
                           BN, TG)
                for c in range(8):
                    U = em.v(lambda v, c=c: v.tensor_tensor(chk(st_f, c),
                                                            chk(st_f, c).bitcast(F32),
                                                            pb[:, 0:BL], ALU.mult), AMM, CARRY, SQMM if c == 0 else None)
                    U2 = em.v(lambda v, c=c: v.tensor_tensor(chk(st_f, c),
                                                             chk(st_f, c).bitcast(F32),
                                                             PX[:, 0:BL], ALU.add), U, BMM)
                    UT = em.v(lambda v, c=c: v.tensor_scalar(chk(st_f, c),
                                                             chk(st_f, c).bitcast(F32),
                                                             lnw_sb[:, c:c + 1], lnb_sb[:, c:c + 1],
                                                             ALU.mult, ALU.add), U2, WL)
                    G2 = em.t(lambda t, c=c: t.matmul(pj[c % 2][:, 0:BL],
                                                      l2_sb[:, 32 + c * 128:32 + (c + 1) * 128],
                                                      tl[:], start=True, stop=True),
                              TG, tk["evac"][c % 2])
                    GS = em.s(lambda s, c=c: s.activation(g_scr[:], pj[c % 2][:, 0:BL], AF.Sigmoid),
                              G2, tk["g_free"], tk["x1t_free"] if c == 0 else None)
                    tk["evac"][c % 2] = GS
                    XNG = em.v(lambda v, c=c: v.tensor_tensor(chk(xng, c), chk(st_f, c).bitcast(F32),
                                                              g_scr[:], ALU.mult), UT, GS, G1)
                    tk["g_free"] = XNG
                    XNGDEP.append(XNG)

                for q in range(BL // 128):
                    DMM = None
                    for n in range(2):
                        for c in range(8):
                            DMM = em.t(lambda t, q=q, n=n, c=c: t.matmul(
                                pd[:, n * 512:(n + 1) * 512],
                                chk(xng, c)[:, q * 128:(q + 1) * 128],
                                Wsb["o"][:, c * 1024 + n * 512:c * 1024 + (n + 1) * 512],
                                start=(c == 0), stop=(c == 7)),
                                XNGDEP[c], tk["pd_free"], DMM)
                    lastX2 = None
                    for n in range(4):
                        X1L = em.y(lambda yq, p=p, t0=t0, q=q, n=n: yq.dma_start(
                            out=x1t[:], in_=x1b[p, t0 + q * 128:t0 + (q + 1) * 128,
                                               n * 256:(n + 1) * 256]),
                            tk["x1t_free"], XNGDEP[-1])
                        X2A = em.v(lambda v, n=n: v.tensor_tensor(
                            x1t[:], x1t[:], pd[:, n * 256:(n + 1) * 256], ALU.add),
                            DMM, X1L, tk["store"], lastX2)
                        ST = em.g(lambda gq, p=p, t0=t0, q=q, n=n: gq.dma_start(
                            out=x2b[p, t0 + q * 128:t0 + (q + 1) * 128, n * 256:(n + 1) * 256],
                            in_=x1t[:]), X2A)
                        tk["x1t_free"] = ST
                        tk["store"] = ST
                        lastX2 = X2A
                    tk["pd_free"] = lastX2

                if dbg and p == 0 and j == 0:
                    D1 = dump("d_rf", r_f[:], EV_ALL[0], EV_ALL[1])
                    D2 = dump("d_kk", kk_f[:].bitcast(F32), EV_ALL[0], EV_ALL[1])
                    D3 = dump("d_vf", v_f[:], EV_ALL[0], EV_ALL[1])
                    D4 = dump("d_xng", st_f[:].bitcast(F32), XW2)
                    dump_sm(0, sm0[:], MU, M2)
                    dump_sm(1, smC[:].bitcast(F32), RSTD)
                    dump_sm(2, smW[:].bitcast(F32), WSM)
                    dump_sm(3, smC[:].bitcast(F32), CSM)
                    dump_sm(5, smW[:].bitcast(F32), BN)
                    tk["dbg"] = ("g", em.cnt["g"])

                tk["mix_free"] = DMM
                tk["sx_free"] = MIXG
                tk["zt_free"] = MIXG
                tk["sm_free"] = BN
                tk["PSA_free"] = MU
                tk["pstB_free"] = VAR
                tk["pb_free"] = UT
                tk["PX_free"] = UT
                tk["kkv_free"] = XNGDEP[-1]
                tk["smW_free"] = WBC
                tk["smB_free"] = BMM
                tk["smC_free"] = AMM
                tk["smR_free"] = None
                tk["PSA_free"] = MU

        em.barrier("g", [("g", em.cnt["g"]), ("y", em.cnt["y"])])
    return nc


def stage_consts(I):
    f32 = np.float32
    PERMQ = np.zeros((8, 128), np.int64)
    for q in range(C):
        h, d = q // 64, q % 64
        PERMQ[d // 8, h * 8 + d % 8] = q
    pf = PERMQ.reshape(-1)

    def pcols(W):
        return np.ascontiguousarray(W[:, pf])

    def prows(W):
        return np.ascontiguousarray(W[pf, :])

    def fvec(v):
        return v[pf].reshape(8, 128)

    out = {}
    out["Wr_p"] = pcols(I["L0_attn_Wr"].astype(f32))
    out["Wk_p"] = pcols(I["L0_attn_Wk"].astype(f32))
    out["Wv_p"] = pcols(I["L0_attn_Wv"].astype(f32))
    out["Wo_p"] = prows(I["L0_attn_Wo"].astype(f32))
    out["w1"] = I["L0_attn_w1"].astype(f32)
    out["a1"] = I["L0_attn_a1"].astype(f32)
    out["g1"] = I["L0_attn_g1"].astype(f32)
    out["w2"] = I["L0_attn_w2"].astype(f32)
    out["a2"] = I["L0_attn_a2"].astype(f32)
    out["g2_p"] = pcols(I["L0_attn_g2"].astype(f32))
    mus = np.zeros((128, 48), f32)
    for i, nm in enumerate(["r", "w", "k", "v", "a", "g"]):
        v = I[f"L0_attn_x_{nm}"][0, 0].astype(f32)
        mus[:, i * 8:(i + 1) * 8] = v.reshape(8, 128).T
    out["mus"] = mus
    out["kkf"] = np.ascontiguousarray(fvec(I["L0_attn_k_k"][0, 0].astype(f32)).T)
    out["w0h"] = I["L0_attn_w0"][0, 0, :, 0].astype(f32).reshape(16, 1)
    out["a0h"] = I["L0_attn_a0"][0, 0, :, 0].astype(f32).reshape(16, 1)
    E16 = np.zeros((16, 128), f32)
    for pp in range(128):
        E16[pp // 8, pp] = 1.0
    out["E16p"] = E16
    out["E16f"] = E16.copy()
    out["sel16"] = np.ascontiguousarray(E16.T)
    rk = I["L0_attn_r_k"].astype(f32)
    rke = np.zeros((128, 8 * 16), f32)
    for c in range(8):
        for pp in range(128):
            h, d = pp // 8, c * 8 + pp % 8
            rke[pp, c * 16 + h] = rk[h, d]
    out["rke"] = rke
    lnw, lnb = fvec(I["L0_attn_ln_w"].astype(f32)), fvec(I["L0_attn_ln_b"].astype(f32))
    out["lnwf"] = np.ascontiguousarray(lnw.T)
    out["lnbf"] = np.ascontiguousarray(lnb.T)
    return out


def stage_core_inputs(core, x1, z1, consts):
    b0 = core * BSH
    x1b = np.ascontiguousarray(x1[b0:b0 + BSH].astype(np.float32))
    z1t = np.zeros((BSH, C, T + 1), np.float32)
    z1t[:, :, 1:] = z1[b0:b0 + BSH].astype(np.float32).transpose(0, 2, 1)
    m = {"x1b": x1b, "z1tb": z1t}
    m.update(consts)
    return m




B, SSH, C, H, D = 16, 512, 1024, 16, 64
NT = 64
SCALE = 0.125



def emit_ln_stats(em, b6, mv, sc4, sc1a, sc1b, x_ap_a, x_ap_b, deps, mv_free=None):
    V = mybir.AxisListType.X
    BN1 = em.v(lambda v: v.bn_stats(b6[:, 0:6], x_ap_a), *deps)
    BN2 = em.v(lambda v: v.bn_stats(b6[:, 6:12], x_ap_b), BN1)
    mview = b6[:].rearrange("p (g k) -> p g k", k=3)[:, :, 1]
    m2view = b6[:].rearrange("p (g k) -> p g k", k=3)[:, :, 2]
    S1 = em.v(lambda v: v.reduce_sum(sc1a[:], mview, axis=V), BN2)
    S2 = em.v(lambda v: v.reduce_sum(sc1b[:], m2view, axis=V), S1)
    MEAN = em.v(lambda v: v.tensor_scalar_mul(mv[:, 0:1], sc1a[:], 0.25), S2, mv_free)
    P = em.v(lambda v: v.tensor_tensor(sc4[:], mview, mview, ALU.mult), MEAN)
    S3 = em.v(lambda v: v.reduce_sum(sc1a[:], sc4[:], axis=V), P)
    V1 = em.v(lambda v: v.tensor_scalar_mul(sc1b[:], sc1b[:], 1.0 / 1024.0), S3)
    V2 = em.v(lambda v: v.scalar_tensor_tensor(sc1b[:], sc1a[:], 0.25, sc1b[:], ALU.mult, ALU.add), V1)
    M2 = em.v(lambda v: v.tensor_tensor(sc1a[:], mv[:, 0:1], mv[:, 0:1], ALU.mult), V2)
    VAR = em.v(lambda v: v.tensor_tensor(mv[:, 1:2], sc1b[:], sc1a[:], ALU.subtract), M2)
    return VAR


def build_pa():
    ctx = contextlib.ExitStack()
    nc = bass.Bass("TRN2", target_bir_lowering=False)
    ap = lambda name, shp, dt, k="ExternalInput": nc.dram_tensor(name, shp, dt, kind=k).ap()
    xs = ap("xs", [B, SSH, C], F32)
    xti = ap("xti", [C, B * SSH], F32R)
    Wgi = ap("Wg", [C, H], F32R)
    Wqi = ap("Wq", [H, C, D], F32)
    Wkvi = ap("Wkv", [H, C, 2 * D], F32)
    bqi = ap("bqT", [D, H], F32)
    bki = ap("bkT", [D, H], F32)
    bvi = ap("bvT", [D, H], F32)
    bdi = ap("bdmask", [128, 128], BF16)
    idfi = ap("idf", [128, 128], F32)
    idbi = ap("idb", [128, 128], BF16)
    G_o = ap("G_o", [NT, 128, 128], BF16, "ExternalOutput")
    Gt_o = ap("Gt_o", [NT, 128, 128], BF16, "ExternalOutput")
    v1_o = ap("v1_o", [D, H * SSH], F32, "ExternalOutput")
    att_o = ap("att_o", [1, H * SSH], F32, "ExternalOutput")
    st_o = ap("st_o", [NT, 128, 2], F32, "ExternalOutput")
    zm_o = ap("zm_o", [1, 2 * H], F32, "ExternalOutput")

    sbuf = lambda name, shp, dt: ctx.enter_context(nc.sbuf_tensor(name, shp, dt))

    Wq_sb = sbuf("Wq_sb", [128, H * 8 * D], BF16)
    Wkv_sb = sbuf("Wkv_sb", [128, H * 8 * 2 * D], BF16)
    Wg_sb = sbuf("Wg_sb", [128, 8 * H], F32R)
    bq_sb = sbuf("bq_sb", [64, H], F32)
    bk_sb = sbuf("bk_sb", [64, H], F32)
    bv_sb = sbuf("bv_sb", [64, H], F32)
    bd_sb = sbuf("bd_sb", [128, 128], BF16)
    idf_sb = sbuf("idf_sb", [128, 128], F32)
    idb_sb = sbuf("idb_sb", [128, 128], BF16)
    ones1 = sbuf("ones1", [128, 1], F32R)
    onesr = sbuf("onesr", [1, 128], F32R)
    cs_row = sbuf("cs_row", [1, H], F32R)
    epsA = sbuf("epsA", [128, 1], F32)
    xt = sbuf("xt", [128, 2 * C], F32)
    xtc = sbuf("xtc", [128, 2 * 1024], F32R)
    z0b = sbuf("z0b", [128, 2 * C], BF16)
    bn6 = sbuf("bn6", [128, 12], F32)
    mv = sbuf("mv", [128, 2], F32)
    sc4s = sbuf("sc4s", [128, 4], F32)
    sc1a = sbuf("sc1a", [128, 1], F32)
    sc1b = sbuf("sc1b", [128, 1], F32)
    stt_ = sbuf("stt_", [128, 2], F32)
    nmr = sbuf("nmr", [128, 1], F32)
    rstd = sbuf("rstd", [128, 1], F32)
    sqv = sbuf("sqv", [128, 1], F32)
    csb = sbuf("csb", [128, H], F32)
    sc0 = sbuf("sc0", [128, H], F32)
    scr = sbuf("scr", [128, H], F32)
    u1 = sbuf("u1", [128, H], F32)
    mx = sbuf("mx", [128, 1], F32)
    lse = sbuf("lse", [128, 1], F32)
    eh = sbuf("eh", [128, H], F32)
    scT = sbuf("scT", [16, 128], F32)
    mxT = sbuf("mxT", [16, 8], F32)
    eqT = sbuf("eqT", [16, 128], BF16)
    eq = sbuf("eq", [128, H], BF16)
    G_t = sbuf("G_t", [128, 2 * 128], BF16)
    Gt_t = sbuf("Gt_t", [128, 2 * 128], BF16)
    hgT = sbuf("hgT", [128, 8 * 2048], BF16)
    qb = sbuf("qb", [64, 128], F32)
    qkr = sbuf("qkr", [64, 128], F32R)
    v1_sb = sbuf("v1_sb", [64, H * SSH], F32)
    att_sb = sbuf("att_sb", [1, H * SSH], F32)
    zm_sb = sbuf("zm_sb", [1, 2 * H], F32)
    exs = sbuf("exs", [1, SSH], F32)

    psum = lambda name, dt=F32: ctx.enter_context(nc.psum_tensor(name, [128, 512], dt))
    pL = psum("pL")
    pT = psum("pT")
    pTb = psum("pTb", BF16)
    pG = psum("pG")
    pQ = psum("pQ")
    pK = psum("pK")
    pA = psum("pA")
    pC = psum("pC")

    with ctx:
        block = ctx.enter_context(nc.Block())
        em = Em(nc, block, ctx)

        def wload(g):
            for h in range(H):
                for c in range(8):
                    g.dma_start(out=Wq_sb[:, (h * 8 + c) * D:(h * 8 + c + 1) * D],
                                in_=Wqi[h, c * 128:(c + 1) * 128, :])
                    g.dma_start(out=Wkv_sb[:, (h * 8 + c) * 2 * D:(h * 8 + c + 1) * 2 * D],
                                in_=Wkvi[h, c * 128:(c + 1) * 128, :])
            for c in range(8):
                g.dma_start(out=Wg_sb[:, c * H:(c + 1) * H], in_=Wgi[c * 128:(c + 1) * 128, :])
            g.dma_start(out=bq_sb[:], in_=bqi)
            g.dma_start(out=bk_sb[:], in_=bki)
            g.dma_start(out=bv_sb[:], in_=bvi)
            g.dma_start(out=bd_sb[:], in_=bdi)
            g.dma_start(out=idf_sb[:], in_=idfi)
            g.dma_start(out=idb_sb[:], in_=idbi)
        WL = em.g(wload)
        ON = em.v(lambda v: v.memset(ones1[:].bitcast(F32), 1.0))
        ON2 = em.v(lambda v: v.memset(onesr[:].bitcast(F32), 1.0))
        EPSI = em.v(lambda v: v.memset(epsA[:], 1e-5))
        CS = None
        for c in range(8):
            CS = em.t(lambda t, c=c: t.matmul(pC[0:1, 0:H], ones1[:, 0:1], Wg_sb[:, c * H:(c + 1) * H],
                                              start=(c == 0), stop=(c == 7)), WL, ON, CS)
        CSE = em.v(lambda v: v.tensor_copy(cs_row[:], pC[0:1, 0:H]), CS)
        CSB0 = em.t(lambda t: t.matmul(pC[:, 64:64 + H], onesr[:], cs_row[:], start=True, stop=True),
                    CSE, ON2)
        CSB = em.s(lambda s: s.copy(csb[:], pC[:, 64:64 + H]), CSB0)

        tkA = {"xt": [None, None], "xtc": [None, None], "z0b": [None, None],
               "G": [None, None], "Gt": [None, None],
               "pL": None, "pT": None, "pTb": None, "pG": None,
               "pQ": None, "pK": None, "pA": None,
               "hg_rd": None, "sc_free": CSB, "eq_free": None, "st_free": None}

        vatt_toks = []
        for i in range(NT):
            bi = i % 2
            isb = i % 16
            sb = i // 16

            XL = em.y(lambda y_, i=i, bi=bi: y_.dma_start(
                out=xt[:, bi * C:(bi + 1) * C],
                in_=xs[:, 8 * i:8 * i + 8, :].transpose([1, 0, 2])), tkA["xt"][bi])

            def xcl(y_, i=i, bi=bi):
                for c in range(8):
                    y_.dma_start(out=xtc[:, bi * 1024 + c * 128:bi * 1024 + (c + 1) * 128],
                                 in_=xti[c * 128:(c + 1) * 128, 128 * i:128 * (i + 1)])
            XC = em.y(xcl, tkA["xtc"][bi])

            AGG = emit_ln_stats(em, bn6, mv, sc4s, sc1a, sc1b,
                                xt[:, bi * C:bi * C + 512], xt[:, bi * C + 512:bi * C + 1024],
                                (XL, tkA["st_free"]), mv_free=tkA.get("mv_free"))
            SQV = em.s(lambda s: s.activation(sqv[:], mv[:, 1:2], AF.Sqrt, bias=epsA[:]), AGG, EPSI)
            tkA["mv_free"] = SQV
            RCP = em.v(lambda v: v.reciprocal(rstd[:], sqv[:]), SQV)
            NMR = em.v(lambda v: v.scalar_tensor_tensor(nmr[:], mv[:, 0:1], -1.0, rstd[:],
                                                        ALU.mult, ALU.mult), RCP)
            STS = em.v(lambda v: v.tensor_copy(stt_[:, 0:1], mv[:, 0:1]), AGG)
            STS2 = em.v(lambda v: v.tensor_copy(stt_[:, 1:2], rstd[:]), RCP, STS)
            STO = em.g(lambda g, i=i: g.dma_start(out=st_o[i], in_=stt_[:]), STS2)
            tkA["st_free"] = STO
            Z0 = em.s(lambda s, bi=bi: s.activation(z0b[:, bi * C:(bi + 1) * C], xt[:, bi * C:(bi + 1) * C],
                                                    AF.Identity, bias=nmr[:], scale=rstd[:]),
                      NMR, tkA["z0b"][bi])

            LMM = None
            for c in range(8):
                LMM = em.t(lambda t, c=c, bi=bi: t.matmul(
                    pL[:, 0:H], xtc[:, bi * 1024 + c * 128:bi * 1024 + (c + 1) * 128],
                    Wg_sb[:, c * H:(c + 1) * H], start=(c == 0), stop=(c == 7)),
                    XC, WL, tkA["pL"], LMM)
            U1 = em.v(lambda v: v.tensor_scalar_mul(u1[:], csb[:], nmr[:]), NMR, CSB, LMM)
            SC0 = em.v(lambda v: v.scalar_tensor_tensor(sc0[:], pL[:, 0:H], rstd[:], u1[:],
                                                        ALU.mult, ALU.add), U1, tkA["sc_free"])
            tkA["pL"] = SC0
            MXx = em.v(lambda v: v.reduce_max(mx[:], sc0[:], axis=mybir.AxisListType.X, negate=True), SC0)
            EH = em.s(lambda s: s.activation(eh[:], sc0[:], AF.Exp, bias=mx[:]), MXx)
            LSE = em.v(lambda v: v.reduce_sum(lse[:], eh[:], axis=mybir.AxisListType.X), EH)
            LLN = em.s(lambda s: s.activation(lse[:], lse[:], AF.Ln), LSE)
            C1 = em.v(lambda v: v.tensor_sub(mx[:], mx[:], lse[:]), LLN)
            SCP = em.v(lambda v: v.tensor_scalar_add(scr[:], sc0[:], mx[:]), C1)
            tkA["sc_free"] = MXx

            TSC = em.t(lambda t: t.transpose(pT[0:16, 0:128], scr[:, 0:H], idf_sb[:]), SCP, tkA["pT"])
            TSE = em.v(lambda v: v.tensor_copy(scT[:], pT[0:16, 0:128]), TSC)
            tkA["pT"] = TSE
            SMX = em.v(lambda v: v.reduce_max(mxT[:], scT[:].rearrange("p (s b) -> p s b", b=16),
                                              axis=mybir.AxisListType.X), TSE)
            EQT = em.v(lambda v: v.tensor_tensor(
                eqT[:].rearrange("p (s b) -> p s b", b=16),
                scT[:].rearrange("p (s b) -> p s b", b=16),
                mxT[:].unsqueeze(2).broadcast_to([16, 8, 16]), ALU.is_equal),
                SMX, tkA["eq_free"])
            TEQ = em.t(lambda t: t.transpose(pTb[0:128, 0:16], eqT[:], idb_sb[0:16, 0:16]),
                       EQT, tkA["pTb"])
            EQE = em.v(lambda v: v.tensor_copy(eq[:], pTb[0:128, 0:16]), TEQ)
            tkA["eq_free"] = EQE
            GB = em.v(lambda v, bi=bi: v.tensor_tensor(
                G_t[:, bi * 128:(bi + 1) * 128].rearrange("p (h j) -> p h j", h=16),
                bd_sb[:].rearrange("p (h j) -> p h j", h=16),
                eq[:].unsqueeze(2).broadcast_to([128, 16, 8]), ALU.mult),
                EQE, tkA["G"][bi])
            TGT = em.t(lambda t, bi=bi: t.transpose(pTb[:, 128:256], G_t[:, bi * 128:(bi + 1) * 128],
                                                    idb_sb[:]), GB, EQE, tkA["pTb"])
            GTE = em.v(lambda v, bi=bi: v.tensor_copy(Gt_t[:, bi * 128:(bi + 1) * 128], pTb[:, 128:256]),
                       TGT, tkA["Gt"][bi])
            tkA["pTb"] = GTE
            GO = em.g(lambda g, i=i, bi=bi: g.dma_start(out=G_o[i], in_=G_t[:, bi * 128:(bi + 1) * 128]), GB)
            GTO = em.g(lambda g, i=i, bi=bi: g.dma_start(out=Gt_o[i], in_=Gt_t[:, bi * 128:(bi + 1) * 128]), GTE)
            tkA["G"][bi] = GO
            tkA["Gt"][bi] = GTO

            GMM = None
            HGE = tkA["pG"]
            for c in range(8):
                GMM = em.t(lambda t, c=c, bi=bi: t.matmul(
                    pG[:, 0:128], z0b[:, bi * C + c * 128:bi * C + (c + 1) * 128],
                    G_t[:, bi * 128:(bi + 1) * 128], start=True, stop=True),
                    GB, Z0, HGE, tkA["hg_rd"] if (isb == 0 and c == 0) else None, GMM)
                HGE = em.s(lambda s, c=c, isb=isb: s.activation(
                    hgT[:, c * 2048:(c + 1) * 2048].rearrange("p (hh n) -> p hh n", hh=16)[:, :, isb * 8:isb * 8 + 8],
                    pG[:, 0:128].rearrange("p (hh j) -> p hh j", hh=16), AF.Copy), GMM)
            tkA["pG"] = HGE
            tkA["z0b"][bi] = GMM
            tkA["xt"][bi] = Z0
            tkA["xtc"][bi] = LMM

            if isb == 15:
                s0 = sb * 128
                prevQK, prevVEV, prevAEV = None, None, None
                for h in range(H):
                    QMM = None
                    for c in range(8):
                        QMM = em.t(lambda t, h=h, c=c: t.matmul(
                            pQ[0:64, 0:128], Wq_sb[:, (h * 8 + c) * D:(h * 8 + c + 1) * D],
                            hgT[:, c * 2048 + h * 128:c * 2048 + (h + 1) * 128],
                            start=(c == 0), stop=(c == 7)),
                            HGE, WL, tkA["pQ"] if c == 0 else None, QMM)
                    QEV = em.s(lambda s, h=h: s.activation(qb[:], pQ[0:64, 0:128], AF.Identity,
                                                           bias=bq_sb[:, h:h + 1]), QMM)
                    tkA["pQ"] = QEV
                    KMM = None
                    for c in range(8):
                        KMM = em.t(lambda t, h=h, c=c: t.matmul(
                            pK[0:128, 0:128], Wkv_sb[:, (h * 8 + c) * 2 * D:(h * 8 + c + 1) * 2 * D],
                            hgT[:, c * 2048 + h * 128:c * 2048 + (h + 1) * 128],
                            start=(c == 0), stop=(c == 7)),
                            HGE, (prevQK if c == 0 else None), (prevVEV if c == 0 else None), KMM)
                    QK = em.v(lambda v, h=h: v.scalar_tensor_tensor(qkr[:], pK[0:64, 0:128],
                                                                    bk_sb[:, h:h + 1], qb[:],
                                                                    ALU.add, ALU.mult), KMM, QEV, prevAEV)
                    VEV = em.s(lambda s, h=h, s0=s0: s.activation(
                        v1_sb[:, h * SSH + s0:h * SSH + s0 + 128], pK[64:128, 0:128], AF.Identity,
                        bias=bv_sb[:, h:h + 1]), KMM)
                    AMM = em.t(lambda t: t.matmul(pA[0:1, 0:128], ones1[0:64, 0:1], qkr[:],
                                                  start=True, stop=True), QK, tkA["pA"], ON)
                    AEV = em.s(lambda s, h=h, s0=s0: s.activation(
                        att_sb[0:1, h * SSH + s0:h * SSH + s0 + 128], pA[0:1, 0:128], AF.Copy,
                        scale=SCALE), AMM)
                    tkA["pA"] = AEV
                    prevQK, prevVEV, prevAEV = QK, VEV, AEV
                    vatt_toks.append(AEV)
                    vatt_toks.append(VEV)
                tkA["hg_rd"] = KMM

        MXA = em.v(lambda v: v.reduce_max(zm_sb[0:1, 0:H], att_sb[:].rearrange("p (h s) -> p h s", h=H),
                                          axis=mybir.AxisListType.X, negate=True),
                   vatt_toks[-1], vatt_toks[-2])
        SUME = None
        for h in range(H):
            EX = em.s(lambda s, h=h: s.activation(exs[:], att_sb[0:1, h * SSH:(h + 1) * SSH], AF.Exp,
                                                  bias=zm_sb[0:1, h:h + 1]), MXA, SUME)
            SUME = em.v(lambda v, h=h: v.reduce_sum(zm_sb[0:1, H + h:H + h + 1], exs[:],
                                                    axis=mybir.AxisListType.X), EX)
        VO = em.g(lambda g: g.dma_start(out=v1_o, in_=v1_sb[:]), *vatt_toks)
        AO = em.g(lambda g: g.dma_start(out=att_o, in_=att_sb[:]), SUME)
        ZMO = em.g(lambda g: g.dma_start(out=zm_o, in_=zm_sb[:]), SUME)
        em.barrier("g", [("g", em.cnt["g"]), ("y", em.cnt["y"])])
    return nc


def build_pb(emit_z=True, tile_barrier=False, serial=True):
    ctx = contextlib.ExitStack()
    nc = bass.Bass("TRN2", target_bir_lowering=False)
    ap = lambda name, shp, dt, k="ExternalInput": nc.dram_tensor(name, shp, dt, kind=k).ap()
    xs = ap("xs", [B, SSH, C], F32)
    Gt_i = ap("Gt_i", [NT, 128, 128], BF16)
    v1_i = ap("v1_i", [D, H * SSH], F32)
    att_i = ap("att_i", [1, H * SSH], F32R)
    st_i = ap("st_i", [NT, 128, 2], F32)
    zmM = ap("zmM", [H, 8], F32)
    zmS = ap("zmS", [H, 8], F32)
    WoFi = ap("WoF", [H, D, C], F32)
    boi = ap("bo", [1, C], F32R)
    lngi = ap("lng", [1, C], F32R)
    lnbi = ap("lnb2", [1, C], F32R)
    phseli = ap("phsel", [H, H * D], F32R)
    x1_o = ap("x1_o", [B, SSH, C], F32, "ExternalOutput")
    z1_o = ap("z1_o", [B, SSH, C], F32, "ExternalOutput") if emit_z else None
    aoD = nc.dram_tensor("aoD", [4, 128, H * C], BF16).ap()

    sbuf = lambda name, shp, dt: ctx.enter_context(nc.sbuf_tensor(name, shp, dt))
    Wo_sb = sbuf("Wo_sb", [64, H * C], BF16)
    bo_bc = sbuf("bo_bc", [128, C], F32)
    gh_bc = sbuf("gh_bc", [128, C], F32)
    bh_bc = sbuf("bh_bc", [128, C], F32)
    onesr = sbuf("onesr", [1, 128], F32R)
    halfr = sbuf("halfr", [1, 128], F32R)
    epsA = sbuf("epsA", [128, 1], F32)
    ph_sb = sbuf("ph_sb", [16, H * D], F32R)
    row_sb = sbuf("row_sb", [1, 3 * C], F32R)
    v1_sb = sbuf("v1_sb", [64, H * SSH], F32)
    p_sb = sbuf("p_sb", [16, SSH], F32R)
    zmt = sbuf("zmt", [16, 24], F32)
    mg = sbuf("mg", [16, 4], F32)
    av_sb = sbuf("av_sb", [64, 2 * SSH], BF16)
    ao_ch = sbuf("ao_ch", [128, C], BF16)
    aoI = sbuf("aoI", [128, 2 * C], BF16)
    Gt_t = sbuf("Gt_t", [128, 2 * 128], BF16)
    xt = sbuf("xt", [128, 2 * C], F32)
    stt_ = sbuf("stt_", [128, 2 * 2], F32)
    z0f = sbuf("z0f", [128, C], F32)
    y_f = sbuf("y_f", [128, C], F32)
    bn6 = sbuf("bn6", [128, 12], F32)
    mv = sbuf("mv", [128, 2], F32)
    sc4s = sbuf("sc4s", [128, 4], F32)
    sc1a = sbuf("sc1a", [128, 1], F32)
    sc1b = sbuf("sc1b", [128, 1], F32)
    sqv = sbuf("sqv", [128, 1], F32)
    rstd2 = sbuf("rstd2", [128, 1], F32)
    nmr2 = sbuf("nmr2", [128, 1], F32)
    nmr0 = sbuf("nmr0", [128, 1], F32)
    zout = sbuf("zout", [128, C], F32)

    pAO = ctx.enter_context(nc.psum_tensor("pAO", [128, 1024], F32))
    pOU = ctx.enter_context(nc.psum_tensor("pOU", [128, 1024], F32))
    pP = ctx.enter_context(nc.psum_tensor("pP", [128, 512], F32))
    pB2 = ctx.enter_context(nc.psum_tensor("pB2", [128, 512], F32))

    with ctx:
        block = ctx.enter_context(nc.Block())
        em = Em(nc, block, ctx, serial=serial)

        def wload(g):
            for h in range(H):
                g.dma_start(out=Wo_sb[:, h * C:(h + 1) * C], in_=WoFi[h])
            g.dma_start(out=v1_sb[:], in_=v1_i)
            g.dma_start(out=zmt[:, 0:8], in_=zmM)
            g.dma_start(out=zmt[:, 8:16], in_=zmS)
            g.dma_start(out=ph_sb[:], in_=phseli)
            g.dma_start(out=p_sb[:], in_=att_i.rearrange("o (h s) -> (o h) s", h=H))
            g.dma_start(out=row_sb[0:1, 0:C], in_=boi)
            g.dma_start(out=row_sb[0:1, C:2 * C], in_=lngi)
            g.dma_start(out=row_sb[0:1, 2 * C:3 * C], in_=lnbi)
        WL = em.g(wload)
        ON2 = em.v(lambda v: v.memset(onesr[:].bitcast(F32), 1.0))
        HF = em.v(lambda v: v.memset(halfr[:].bitcast(F32), 0.5))
        EPSI = em.v(lambda v: v.memset(epsA[:], 1e-5))

        prev = None
        for gi, (dst, lz) in enumerate([(bo_bc, onesr), (gh_bc, halfr), (bh_bc, halfr)]):
            for n in range(2):
                M1 = em.t(lambda t, gi=gi, lz=lz, n=n: t.matmul(
                    pB2[:, 0:512], lz[:], row_sb[0:1, gi * C + n * 512:gi * C + (n + 1) * 512],
                    start=True, stop=True), ON2, HF, WL, prev)
                prev = em.s(lambda s, dst=dst, n=n: s.copy(dst[:, n * 512:(n + 1) * 512], pB2[:, 0:512]), M1)
        BCD = prev

        GM2 = em.v(lambda v: v.tensor_reduce(mg[:, 0:1], zmt[:, 0:8], axis=mybir.AxisListType.X,
                                             op=ALU.min), WL)
        ECd = em.v(lambda v: v.scalar_tensor_tensor(zmt[:, 16:24], zmt[:, 0:8], -1.0,
                                                    mg[:, 0:1].broadcast_to([16, 8]),
                                                    ALU.mult, ALU.add), GM2)
        EC = em.s(lambda s: s.activation(zmt[:, 16:24], zmt[:, 16:24], AF.Exp), ECd)
        ZZ = em.v(lambda v: v.tensor_tensor(zmt[:, 16:24], zmt[:, 16:24], zmt[:, 8:16], ALU.mult), EC)
        ZG = em.v(lambda v: v.reduce_sum(mg[:, 1:2], zmt[:, 16:24], axis=mybir.AxisListType.X), ZZ)
        RZ = em.v(lambda v: v.reciprocal(mg[:, 2:3], mg[:, 1:2]), ZG)
        PE1 = em.s(lambda s: s.activation(p_sb[:], p_sb[:].bitcast(F32), AF.Exp,
                                          bias=mg[:, 0:1]), WL, GM2)
        PE2 = em.v(lambda v: v.tensor_scalar_mul(p_sb[:], p_sb[:].bitcast(F32), mg[:, 2:3]), PE1, RZ)

        prevAV = [None, None]
        prevPBC = None
        prevAOS = {}
        tk_pAO = None
        for h in range(H):
            bi = h % 2
            PBC = em.t(lambda t, h=h: t.matmul(pP[0:64, 0:SSH], ph_sb[:, h * D:(h + 1) * D],
                                               p_sb[:], start=True, stop=True),
                       PE2, WL, BCD, prevPBC)
            AV = em.v(lambda v, h=h, bi=bi: v.tensor_tensor(
                av_sb[:, bi * SSH:(bi + 1) * SSH], v1_sb[:, h * SSH:(h + 1) * SSH],
                pP[0:64, 0:SSH], ALU.mult), PBC, WL, prevAV[bi])
            prevPBC = AV
            lastAOM = None
            for q in range(4):
                AOM = None
                for n in range(2):
                    AOM = em.t(lambda t, h=h, q=q, n=n, bi=bi: t.matmul(
                        pAO[:, n * 512:(n + 1) * 512],
                        av_sb[:, bi * SSH + q * 128:bi * SSH + q * 128 + 128],
                        Wo_sb[:, h * C + n * 512:h * C + (n + 1) * 512],
                        start=True, stop=True), AV, tk_pAO if n == 0 else None, AOM)
                AOE = em.v(lambda v, h=h, q=q: v.tensor_tensor(
                    ao_ch[:], pAO[:, 0:1024], bo_bc[:], ALU.add),
                    AOM, BCD, prevAOS.get(0))
                tk_pAO = AOE
                AOS = em.g(lambda g, h=h, q=q: g.dma_start(
                    out=aoD[q, :, h * C:(h + 1) * C], in_=ao_ch[:]), AOE)
                prevAOS[0] = AOS
                lastAOM = AOM
            prevAV[bi] = lastAOM
        AOALL = prevAOS[0]

        tkB = {"aoI": [None, None], "Gt": [None, None], "xt": [None, None],
               "pOU": None, "y": None, "zo": None}
        STL = [None, None]

        for i in range(NT):
            bi = i % 2
            q = i // 16
            if tile_barrier:
                snap = [(e, em.cnt[e]) for e in ("t", "v", "s", "y", "g")]
                for e in ("t", "v", "s", "y", "g"):
                    em.barrier(e, snap)
            XL = em.y(lambda y_, i=i, bi=bi: y_.dma_start(
                out=xt[:, bi * C:(bi + 1) * C],
                in_=xs[:, 8 * i:8 * i + 8, :].transpose([1, 0, 2])), tkB["xt"][bi])
            STLD = em.y(lambda y_, i=i, bi=bi: y_.dma_start(
                out=stt_[:, bi * 2:bi * 2 + 2], in_=st_i[i]), STL[bi])
            GTL = em.y(lambda y_, i=i, bi=bi: y_.dma_start(
                out=Gt_t[:, bi * 128:(bi + 1) * 128], in_=Gt_i[i]), tkB["Gt"][bi])
            AIL = em.y(lambda y_, i=i, q=q, bi=bi: y_.dma_start(
                out=aoI[:, bi * C:(bi + 1) * C],
                in_=aoD[q].rearrange("r (h d) -> r h d", h=H)[8 * (i % 16):8 * (i % 16) + 8]
                    .transpose([1, 0, 2])), tkB["aoI"][bi], AOALL)
            SCM = None
            for n in range(2):
                SCM = em.t(lambda t, n=n, bi=bi: t.matmul(
                    pOU[:, n * 512:(n + 1) * 512], Gt_t[:, bi * 128:(bi + 1) * 128],
                    aoI[:, bi * C + n * 512:bi * C + (n + 1) * 512],
                    start=True, stop=True), GTL, AIL, tkB["pOU"] if n == 0 else None, SCM)
            NM2 = em.v(lambda v, bi=bi: v.scalar_tensor_tensor(nmr0[:], stt_[:, bi * 2:bi * 2 + 1], -1.0,
                                                               stt_[:, bi * 2 + 1:bi * 2 + 2],
                                                               ALU.mult, ALU.mult), STLD)
            Z0F = em.s(lambda s, bi=bi: s.activation(z0f[:], xt[:, bi * C:(bi + 1) * C], AF.Identity,
                                                     scale=stt_[:, bi * 2 + 1:bi * 2 + 2],
                                                     bias=nmr0[:]), NM2, XL, tkB["y"])
            YT = em.v(lambda v: v.tensor_tensor(y_f[:], z0f[:], pOU[:, 0:1024], ALU.add), Z0F, SCM)
            tkB["pOU"] = YT
            AGG = emit_ln_stats(em, bn6, mv, sc4s, sc1a, sc1b,
                                y_f[:, 0:512], y_f[:, 512:1024], (YT,), mv_free=tkB.get("mv_free"))
            SQV = em.s(lambda s: s.activation(sqv[:], mv[:, 1:2], AF.Sqrt, bias=epsA[:]), AGG, EPSI)
            tkB["mv_free"] = SQV
            RCP = em.v(lambda v: v.reciprocal(rstd2[:], sqv[:]), SQV)
            NMR = em.v(lambda v: v.scalar_tensor_tensor(nmr2[:], mv[:, 0:1], -1.0, rstd2[:],
                                                        ALU.mult, ALU.mult), RCP)
            T2 = em.s(lambda s: s.activation(z0f[:], y_f[:], AF.Identity, scale=rstd2[:], bias=nmr2[:]),
                      NMR, YT)
            U = em.v(lambda v: v.tensor_tensor(y_f[:], z0f[:], gh_bc[:], ALU.mult), T2, BCD)
            X1A = em.v(lambda v, bi=bi: v.tensor_tensor(xt[:, bi * C:(bi + 1) * C],
                                                        xt[:, bi * C:(bi + 1) * C], y_f[:], ALU.add),
                       U, Z0F)
            X1B = em.v(lambda v, bi=bi: v.tensor_tensor(xt[:, bi * C:(bi + 1) * C],
                                                        xt[:, bi * C:(bi + 1) * C], bh_bc[:], ALU.add),
                       X1A, BCD)
            XO = em.g(lambda g, i=i, bi=bi: g.dma_start(
                out=x1_o[:, 8 * i:8 * i + 8, :].transpose([1, 0, 2]),
                in_=xt[:, bi * C:(bi + 1) * C]), X1B)
            tkB["y"] = X1B
            STL[bi] = NM2
            tkB["Gt"][bi] = SCM
            tkB["aoI"][bi] = SCM
            tkB["xt"][bi] = XO
            if emit_z:
                AGGz = emit_ln_stats(em, bn6, mv, sc4s, sc1a, sc1b,
                                     xt[:, bi * C:bi * C + 512], xt[:, bi * C + 512:bi * C + 1024],
                                     (X1B,), mv_free=tkB.get("mv_free"))
                SQVz = em.s(lambda s: s.activation(sqv[:], mv[:, 1:2], AF.Sqrt, bias=epsA[:]), AGGz, T2)
                tkB["mv_free"] = SQVz
                RCPz = em.v(lambda v: v.reciprocal(rstd2[:], sqv[:]), SQVz)
                NMRz = em.v(lambda v: v.scalar_tensor_tensor(nmr2[:], mv[:, 0:1], -1.0, rstd2[:],
                                                             ALU.mult, ALU.mult), RCPz)
                ZA = em.s(lambda s, bi=bi: s.activation(zout[:], xt[:, bi * C:(bi + 1) * C], AF.Identity,
                                                        scale=rstd2[:], bias=nmr2[:]),
                          NMRz, tkB["zo"])
                ZO = em.g(lambda g, i=i: g.dma_start(
                    out=z1_o[:, 8 * i:8 * i + 8, :].transpose([1, 0, 2]), in_=zout[:]), ZA)
                tkB["zo"] = ZO
                tkB["y"] = ZA
                STL[bi] = NMRz
                tkB["xt"][bi] = ZO
        em.barrier("g", [("g", em.cnt["g"]), ("y", em.cnt["y"])])
    return nc


NCORES = 8
TFULL = 4096

_progs = {}

def _get(name, fn):
    if name not in _progs:
        _progs[name] = fn()
    return _progs[name]


def _stage_pa_consts(I):
    f32 = np.float32
    BFD = ml_dtypes.bfloat16
    bq = I["L0_ffn_bq"].astype(f32)
    bkv = I["L0_ffn_bkv"].astype(f32)
    bd = np.zeros((128, 128), BFD)
    for sj in range(8):
        bd[16 * sj:16 * sj + 16, sj::8] = 1
    return {
        "Wg": np.ascontiguousarray(I["L0_ffn_Wg"].astype(f32)),
        "Wq": np.ascontiguousarray(I["L0_ffn_Wq"].astype(f32)),
        "Wkv": np.ascontiguousarray(I["L0_ffn_Wkv"].astype(f32)),
        "bqT": np.ascontiguousarray(bq.T),
        "bkT": np.ascontiguousarray(bkv[:, :D].T),
        "bvT": np.ascontiguousarray(bkv[:, D:].T),
        "bdmask": bd,
        "idf": np.eye(128, dtype=f32),
        "idb": np.eye(128).astype(BFD),
    }


def _stage_pb_consts(I):
    f32 = np.float32
    return {
        "WoF": np.ascontiguousarray(I["L0_ffn_Wo"].astype(f32)),
        "bo": np.ascontiguousarray(I["L0_ffn_bo"].astype(f32).reshape(1, C)),
        "lng": np.ascontiguousarray(I["L0_ffn_ln_g"].astype(f32).reshape(1, C)),
        "lnb2": np.ascontiguousarray(I["L0_ffn_ln_b"].astype(f32).reshape(1, C)),
        "phsel": np.kron(np.eye(16, dtype=f32), np.ones((1, 64), f32)),
    }


def _run_pa(xfull, cA):
    nc = _get("pa", build_pa)
    maps = []
    for core in range(NCORES):
        xs = np.ascontiguousarray(xfull[:, core * SSH:(core + 1) * SSH, :])
        xti = np.ascontiguousarray(xs.transpose(1, 0, 2).reshape(B * SSH, C).T)
        m = {"xs": xs, "xti": xti}
        m.update(cA)
        maps.append(m)
    return run_bass_kernel_spmd(nc, maps, core_ids=list(range(NCORES))).results


def _run_pb(xfull, res_a, cB, emit_z):
    nc = _get("pb_z" if emit_z else "pb", lambda: build_pb(emit_z=emit_z, tile_barrier=True))
    zmM = np.ascontiguousarray(np.stack([res_a[c]["zm_o"][0, 0:16] for c in range(NCORES)], 1))
    zmS = np.ascontiguousarray(np.stack([res_a[c]["zm_o"][0, 16:32] for c in range(NCORES)], 1))
    maps = []
    for core in range(NCORES):
        xs = np.ascontiguousarray(xfull[:, core * SSH:(core + 1) * SSH, :])
        m = {"xs": xs, "Gt_i": res_a[core]["Gt_o"], "v1_i": res_a[core]["v1_o"],
             "att_i": res_a[core]["att_o"], "st_i": res_a[core]["st_o"],
             "zmM": zmM, "zmS": zmS}
        m.update(cB)
        maps.append(m)
    return run_bass_kernel_spmd(nc, maps, core_ids=list(range(NCORES))).results


def _run_pc(x1full, z1full_bf, cC):
    nc = _get("pc", build_pc)
    maps = []
    for core in range(NCORES):
        b0 = core * BSH
        x1b = np.ascontiguousarray(x1full[b0:b0 + BSH])
        z1t = np.zeros((BSH, C, TFULL + 1), np.float32)
        z1t[:, :, 1:] = z1full_bf[b0:b0 + BSH].transpose(0, 2, 1).astype(np.float32)
        m = {"x1b": x1b, "z1tb": z1t}
        m.update(cC)
        maps.append(m)
    return run_bass_kernel_spmd(nc, maps, core_ids=list(range(NCORES))).results


def kernel(**inputs):
    I = {k: np.asarray(v) for k, v in inputs.items()}
    x = np.ascontiguousarray(I["L0_attn_x"].astype(np.float32))
    cA = _stage_pa_consts(I)
    cB = _stage_pb_consts(I)
    cC = stage_consts(I)

    ra1 = _run_pa(x, cA)
    rb1 = _run_pb(x, ra1, cB, emit_z=True)
    x1 = np.concatenate([rb1[c]["x1_o"] for c in range(NCORES)], 1)
    z1 = np.concatenate([rb1[c]["z1_o"] for c in range(NCORES)], 1)
    rc = _run_pc(x1, z1, cC)
    x2 = np.concatenate([rc[c]["x2b"] for c in range(NCORES)], 0)
    ra2 = _run_pa(x2, cA)
    rb2 = _run_pb(x2, ra2, cB, emit_z=False)
    x3 = np.concatenate([rb2[c]["x1_o"] for c in range(NCORES)], 1)
    return x3


# version 10 (final)
# speedup vs baseline: 1.3581x; 1.3581x over previous
import contextlib
import numpy as np
import ml_dtypes
import concourse.bass as bass
import concourse.mybir as mybir
from concourse.bass_utils import run_bass_kernel_spmd

F32 = mybir.dt.float32
F32R = mybir.dt.float32
BF16 = mybir.dt.bfloat16
AF = mybir.ActivationFunctionType
ALU = mybir.AluOpType

BSH, T, C, H, D = 2, 4096, 1024, 16, 64
NBLK, BL = 16, 256
EXPHALF = float(np.exp(-0.5))
ENG_NAME = {"t": "tensor", "v": "vector", "s": "scalar", "g": "gpsimd", "y": "sync"}


class Em:
    def __init__(self, nc, block, ctx, serial=False):
        self.nc, self.block = nc, block
        self.serial = serial
        self.sems = {}
        self.cnt = {e: 0 for e in ENG_NAME}
        for e in ENG_NAME:
            self.sems[e] = ctx.enter_context(nc.semaphore(f"sem_{e}"))

    def emit(self, eng, fn, deps, dma=False):
        merged = {}
        if self.serial:
            deps = list(deps) + [(e, self.cnt[e]) for e in ENG_NAME]
        for d in deps:
            if d is None:
                continue
            e, c = d
            if c > 0:
                merged[e] = max(merged.get(e, 0), c)
        sems = self.sems
        dec = getattr(self.block, ENG_NAME[eng])
        ndma = [0]

        class _Px:
            def __init__(px, h): px.h = h
            def dma_start(px, **kw):
                inst = px.h.dma_start(**kw)
                inst.then_inc(sems[eng], 16)
                ndma[0] += 1
                return inst
            def __getattr__(px, a): return getattr(px.h, a)

        @dec
        def _(h):
            for e, c in sorted(merged.items()):
                h.wait_ge(sems[e], c)
            if dma:
                fn(_Px(h))
            else:
                last = fn(h)
                assert last is not None
                last.then_inc(sems[eng], 1)
        self.cnt[eng] += 16 * ndma[0] if dma else 1
        return (eng, self.cnt[eng])

    def t(self, fn, *deps): return self.emit("t", fn, deps)
    def v(self, fn, *deps): return self.emit("v", fn, deps)
    def s(self, fn, *deps): return self.emit("s", fn, deps)
    def g(self, fn, *deps): return self.emit("g", fn, deps, dma=True)
    def y(self, fn, *deps): return self.emit("y", fn, deps, dma=True)

    def barrier(self, eng, deps):
        merged = {}
        for d in deps:
            if d is None:
                continue
            e, c = d
            merged[e] = max(merged.get(e, 0), c)
        sems = self.sems
        dec = getattr(self.block, ENG_NAME[eng])

        @dec
        def _(h):
            for e, c in sorted(merged.items()):
                h.wait_ge(sems[e], c)


def build_pc(dbg=False, serial=True, blk_barrier=False):
    ctx = contextlib.ExitStack()
    nc = bass.Bass("TRN2", target_bir_lowering=False)
    ap = lambda name, shp, dt, k="ExternalInput": nc.dram_tensor(name, shp, dt, kind=k).ap()
    x1b = ap("x1b", [BSH, T, C], F32)
    z1tb = ap("z1tb", [BSH, C, T + 1], F32)
    x2b = ap("x2b", [BSH, T, C], F32, "ExternalOutput")
    dbg_out = {}
    if dbg:
        for nm in ["d_mixr", "d_rf", "d_kk", "d_vf", "d_wbc", "d_kv", "d_st", "d_xng", "d_sm"]:
            dbg_out[nm] = ap(nm, [128, 8 * BL], F32, "ExternalOutput")
    Wi = {n: ap(n, [C, C], F32R) for n in ["Wr_p", "Wk_p", "Wv_p", "Wo_p"]}
    w1i, a1i, g1i = ap("w1", [C, 64], F32R), ap("a1", [C, 64], F32R), ap("g1", [C, 64], F32R)
    w2i, a2i = ap("w2", [64, 16], F32R), ap("a2", [64, 16], F32R)
    g2i = ap("g2_p", [64, C], F32R)
    musi = ap("mus", [128, 48], F32)
    kkfi = ap("kkf", [128, 8], F32)
    w0i, a0i = ap("w0h", [16, 1], F32), ap("a0h", [16, 1], F32)
    E16i = ap("E16p", [16, 128], F32R)
    E16fi = ap("E16f", [16, 128], F32)
    seli = ap("sel16", [128, 16], F32R)
    rkei = ap("rke", [128, 8 * 16], F32R)
    lnwi = ap("lnwf", [128, 8], F32)
    lnbi = ap("lnbf", [128, 8], F32)

    sbuf = lambda name, shp, dt: ctx.enter_context(nc.sbuf_tensor(name, shp, dt))
    psum = lambda name, shp: ctx.enter_context(nc.psum_tensor(name, shp, F32))

    Wsb = {n: sbuf(f"W{n}_sb", [128, 8 * 1024], F32R) for n in ["r", "k", "v", "o"]}
    l1_sb = sbuf("l1_sb", [128, 3 * 512], F32R)
    l2_sb = sbuf("l2_sb", [64, 32 + 1024], F32R)
    mus_sb = sbuf("mus_sb", [128, 48], F32)
    kkf_sb = sbuf("kkf_sb", [128, 8], F32)
    w0_sb = sbuf("w0_sb", [16, 1], F32)
    a0_sb = sbuf("a0_sb", [16, 1], F32)
    E16_sb = sbuf("E16_sb", [16, 128], F32R)
    E16f_sb = sbuf("E16f_sb", [16, 128], F32)
    sel_sb = sbuf("sel_sb", [128, 16], F32R)
    rke_sb = sbuf("rke_sb", [128, 8 * 16], F32R)
    lnw_sb = sbuf("lnw_sb", [128, 8], F32)
    lnb_sb = sbuf("lnb_sb", [128, 8], F32)
    zt = sbuf("zt", [128, 8 * (BL + 1)], F32)
    sx = sbuf("sx", [128, 8 * BL], F32)
    mixb = sbuf("mixb", [128, 8 * BL], F32R)
    r_f = sbuf("r_f", [128, 8 * BL], F32)
    kk_f = sbuf("kk_f", [128, 8 * BL], F32R)
    v_f = sbuf("v_f", [128, 8 * BL], F32)
    st_f = sbuf("st_f", [128, 8 * BL], F32R)
    wbc = sbuf("wbc", [128, BL], F32)
    sq_f = sbuf("sq_f", [128, BL], F32R)
    g_scr = sbuf("g_scr", [128, BL], F32)
    tl = sbuf("tl", [64, BL], F32R)
    xng = mixb
    carry = sbuf("carry", [128, 8], F32)
    sm0 = sbuf("sm0", [16, BL], F32)
    sm1 = sbuf("sm1", [16, BL], F32)
    smI = sbuf("smI", [16, BL], F32R)
    smW = sbuf("smW", [16, BL], F32)
    smB = sbuf("smB", [16, BL], F32R)
    smC = sbuf("smC", [16, BL], F32R)
    eps16 = sbuf("eps16", [16, 1], F32)
    x1t = g_scr

    pj = [psum("pj0", [128, 512]), psum("pj1", [128, 512])]
    pb = psum("pb", [128, 512])
    PX = psum("PX", [128, 512])
    PSA = psum("PSA", [16, 512])
    pstB = psum("pstB", [16, 512])
    pd = psum("pd", [128, 1024])

    pl64 = PX[0:64, 0:BL]
    pls16 = PSA[0:16, 0:BL]

    with ctx:
        block = ctx.enter_context(nc.Block())
        em = Em(nc, block, ctx, serial=serial)

        def wload(g):
            last = None
            for k in range(8):
                for n, src in [("r", Wi["Wr_p"]), ("k", Wi["Wk_p"]), ("v", Wi["Wv_p"]), ("o", Wi["Wo_p"])]:
                    last = g.dma_start(out=Wsb[n][:, k * 1024:(k + 1) * 1024],
                                       in_=src[k * 128:(k + 1) * 128, :])
                last = g.dma_start(out=l1_sb[:, k * 64:(k + 1) * 64], in_=w1i[k * 128:(k + 1) * 128, :])
                last = g.dma_start(out=l1_sb[:, 512 + k * 64:512 + (k + 1) * 64], in_=a1i[k * 128:(k + 1) * 128, :])
                last = g.dma_start(out=l1_sb[:, 1024 + k * 64:1024 + (k + 1) * 64], in_=g1i[k * 128:(k + 1) * 128, :])
            last = g.dma_start(out=l2_sb[:, 0:16], in_=w2i)
            last = g.dma_start(out=l2_sb[:, 16:32], in_=a2i)
            last = g.dma_start(out=l2_sb[:, 32:32 + 1024], in_=g2i)
            last = g.dma_start(out=mus_sb[:], in_=musi)
            last = g.dma_start(out=kkf_sb[:], in_=kkfi)
            last = g.dma_start(out=w0_sb[:], in_=w0i)
            last = g.dma_start(out=a0_sb[:], in_=a0i)
            last = g.dma_start(out=E16_sb[:], in_=E16i)
            last = g.dma_start(out=E16f_sb[:], in_=E16fi)
            last = g.dma_start(out=sel_sb[:], in_=seli)
            last = g.dma_start(out=rke_sb[:], in_=rkei)
            last = g.dma_start(out=lnw_sb[:], in_=lnwi)
            last = g.dma_start(out=lnb_sb[:], in_=lnbi)
            return last
        WL = em.g(wload)
        EPS = em.v(lambda v: v.memset(eps16[:], 1e-5))

        chk = lambda buf, c, n=BL: buf[:, c * n:(c + 1) * n]

        def dump(nm, src_ap, *deps):
            if not dbg or nm not in dbg_out:
                return None
            n = src_ap.shape[-1]
            return em.g(lambda g: g.dma_start(out=dbg_out[nm][:, 0:n], in_=src_ap), *deps)

        def dump_sm(col, src_ap, *deps):
            if not dbg:
                return None
            return em.g(lambda g: g.dma_start(out=dbg_out["d_sm"][0:16, col * BL:(col + 1) * BL],
                                              in_=src_ap), *deps)

        tk = {
            "mix_free": None, "sx_free": None, "zt_free": None,
            "evac": [None, None],
            "pb_free": None, "PX_free": None, "PSA_free": None, "pstB_free": None,
            "pd_free": None, "x1t_free": None, "sm_free": None,
            "kkv_free": None, "smW_free": None, "smC_free": None, "smR_free": None, "smB_free": None, "g_free": None,
            "store": None, "dbg": None,
        }

        for p in range(BSH):
            CR = em.v(lambda v: v.memset(carry[:], 0.0), tk["kkv_free"])
            for j in range(NBLK):
                t0 = j * BL

                def zload(yq, p=p, t0=t0):
                    last = None
                    for c in range(8):
                        last = yq.dma_start(out=chk(zt, c, BL + 1),
                                            in_=z1tb[p, c * 128:(c + 1) * 128, t0:t0 + BL + 1])
                    return last
                ZL = em.y(zload, tk["zt_free"])

                def mk_sx(v):
                    last = None
                    for c in range(8):
                        zc = chk(zt, c, BL + 1)
                        last = v.tensor_tensor(chk(sx, c), zc[:, 1:1 + BL], zc[:, 0:BL], ALU.subtract)
                    return last
                SX = em.v(mk_sx, ZL, tk["sx_free"], tk["dbg"])

                def mix(name, *deps):
                    mcol = {"r": 0, "w": 1, "k": 2, "v": 3, "a": 4, "g": 5}[name]

                    def f(v, mcol=mcol):
                        last = None
                        for c in range(8):
                            zc = chk(zt, c, BL + 1)
                            last = v.scalar_tensor_tensor(
                                chk(mixb, c), chk(sx, c),
                                mus_sb[:, mcol * 8 + c:mcol * 8 + c + 1],
                                zc[:, 1:1 + BL], ALU.mult, ALU.add)
                        return last
                    return em.v(f, SX, WL, *deps)

                def proj(wname, evac_fn, MIX, extra=()):
                    MM = None
                    for m in range(8):
                        def mm(t, m=m, wname=wname):
                            last = None
                            for k in range(8):
                                last = t.matmul(pj[m % 2][:, 0:BL],
                                                Wsb[wname][:, k * 1024 + m * 128:k * 1024 + m * 128 + 128],
                                                chk(mixb, k), start=(k == 0), stop=(k == 7))
                            return last
                        MM = em.t(mm, MIX, tk["evac"][m % 2], *extra)
                        ed = (tk["kkv_free"],) if m == 0 else ()
                        E = em.s(lambda s, m=m: evac_fn(s, m), MM, *ed)
                        tk["evac"][m % 2] = E
                    return MM

                MIXR = mix("r", tk["mix_free"], tk["kkv_free"])
                MMR = proj("r", lambda s, m: s.copy(chk(r_f, m), pj[m % 2][:, 0:BL]), MIXR)
                RL = tk["evac"][1]

                MIXK = mix("k", MMR)
                MMK = proj("k", lambda s, m: s.activation(
                    chk(kk_f, m), pj[m % 2][:, 0:BL], AF.Copy,
                    scale=kkf_sb[:, m:m + 1]), MIXK)

                N2 = None
                SQ = None
                for c in range(8):
                    SQ = em.s(lambda s, c=c: s.activation(sq_f[:],
                                                          chk(kk_f, c).bitcast(F32), AF.Square),
                              tk["evac"][0], tk["evac"][1], N2)
                    N2 = em.t(lambda t, c=c: t.matmul(pls16, sel_sb[:], sq_f[:],
                                                      start=(c == 0), stop=(c == 7)),
                              SQ, WL, tk["PSA_free"])
                MX = em.v(lambda v: v.tensor_scalar_max(sm0[:], pls16, 1e-24), N2, tk["sm_free"])
                LNN = em.s(lambda s: s.activation(sm1[:], sm0[:], AF.Ln), MX)
                INV = em.s(lambda s: s.activation(smI[:], sm1[:], AF.Exp, scale=-0.5), LNN)

                MIXV = mix("v", MMK)
                MMV = proj("v", lambda s, m: s.copy(chk(v_f, m), pj[m % 2][:, 0:BL]), MIXV)
                EV_ALL = (tk["evac"][0], tk["evac"][1])

                def mml(t, off):
                    last = None
                    for k in range(8):
                        last = t.matmul(pl64, l1_sb[:, off + k * 64:off + (k + 1) * 64],
                                        chk(mixb, k), start=(k == 0), stop=(k == 7))
                    return last
                MIXW = mix("w", MMV)
                W1 = em.t(lambda t: mml(t, 0), MIXW, tk["PX_free"])
                TW = em.s(lambda s: s.activation(tl[:], pl64, AF.Tanh), W1)
                W2 = em.t(lambda t: t.matmul(pls16, l2_sb[:, 0:16], tl[:], start=True, stop=True),
                          TW, MX)
                WSG = em.s(lambda s: s.activation(sm1[:], pls16, AF.Sigmoid, bias=w0_sb[:]), W2, INV)
                WSM = em.v(lambda v: v.tensor_scalar_mul(smW[:], sm1[:], EXPHALF), WSG, tk["smW_free"])

                MIXA = mix("a", W1)
                A1 = em.t(lambda t: mml(t, 512), MIXA, TW)
                TA = em.s(lambda s: s.activation(tl[:], pl64, AF.Tanh), A1, W2)
                A2 = em.t(lambda t: t.matmul(pls16, l2_sb[:, 16:32], tl[:], start=True, stop=True),
                          TA, WSG)
                ASG = em.s(lambda s: s.activation(sm1[:], pls16, AF.Sigmoid, bias=a0_sb[:]), A2, WSM)
                CSM = em.v(lambda v: v.tensor_tensor(smC[:], sm1[:], smI[:].bitcast(F32), ALU.mult),
                           ASG, INV, tk["smC_free"])

                WBC = em.t(lambda t: t.matmul(pb[:, 0:BL], E16f_sb[:], smW[:], start=True, stop=True),
                           WSM, WL, tk["pb_free"])
                WBE = em.s(lambda s: s.copy(wbc[:], pb[:, 0:BL]), WBC)
                D_WBC = dump("d_wbc", wbc[:], WBE) if (p == 0 and j == 0) else None
                CBC = em.t(lambda t: t.matmul(PX[:, 0:BL], E16_sb[:], smC[:], start=True, stop=True),
                           CSM, TA)

                def mk_kv1(v):
                    last = None
                    for c in range(8):
                        last = v.tensor_tensor(chk(st_f, c), chk(kk_f, c).bitcast(F32),
                                               chk(v_f, c), ALU.mult)
                    return last
                KV1 = em.v(mk_kv1, MMV, tk["evac"][0], tk["evac"][1], CSM)

                def mk_kv2(v):
                    last = None
                    for c in range(8):
                        last = v.tensor_tensor(chk(st_f, c), chk(st_f, c).bitcast(F32),
                                               PX[:, 0:BL], ALU.mult)
                    return last
                KV2 = em.v(mk_kv2, KV1, CBC)
                D_KV = dump("d_kv", st_f[:].bitcast(F32), KV2) if (p == 0 and j == 0) else None

                def mk_scan(v):
                    last = None
                    for c in range(8):
                        last = v.tensor_tensor_scan(chk(st_f, c), wbc[:], chk(st_f, c).bitcast(F32),
                                                    carry[:, c:c + 1], ALU.mult, ALU.add)
                    return last
                SCN = em.v(mk_scan, KV2, WBE, CR, D_KV)

                def mk_carry(v):
                    last = None
                    for c in range(8):
                        last = v.tensor_copy(carry[:, c:c + 1], chk(st_f, c).bitcast(F32)[:, BL - 1:BL])
                    return last
                CARRY = em.v(mk_carry, SCN)
                D_ST = dump("d_st", st_f[:].bitcast(F32), CARRY) if (p == 0 and j == 0) else None

                INMM = None
                RK1 = None
                for c in range(8):
                    RK1 = em.v(lambda v, c=c: v.tensor_tensor(sq_f[:], chk(r_f, c),
                                                              chk(kk_f, c).bitcast(F32), ALU.mult),
                               EV_ALL[0], EV_ALL[1], INMM, N2, KV2)
                    INMM = em.t(lambda t, c=c: t.matmul(pls16, rke_sb[:, c * 16:(c + 1) * 16],
                                                        sq_f[:], start=(c == 0), stop=(c == 7)),
                                RK1, ASG if c == 0 else None)
                INEV = em.v(lambda v: v.tensor_tensor(smC[:], pls16, smI[:].bitcast(F32), ALU.mult),
                            INMM, CBC, CSM)
                D_IN = dump_sm(4, smC[:].bitcast(F32), INEV) if (p == 0 and j == 0) else None
                BCMM = em.t(lambda t: t.matmul(pb[:, 0:BL], E16_sb[:], smC[:], start=True, stop=True),
                            INEV, WBE)
                def mk_xw1(v):
                    last = None
                    for c in range(8):
                        last = v.tensor_tensor(chk(st_f, c), chk(st_f, c).bitcast(F32),
                                               chk(r_f, c), ALU.mult)
                    return last
                XW1 = em.v(mk_xw1, SCN, CARRY, D_KV, D_ST)
                XW2 = None
                for c in range(8):
                    BONC = em.v(lambda v, c=c: v.tensor_tensor(wbc[:], chk(v_f, c), pb[:, 0:BL], ALU.mult),
                                BCMM, XW1 if c == 0 else XW2, SCN if c == 0 else None, D_WBC)
                    XW2 = em.v(lambda v, c=c: v.tensor_tensor(chk(st_f, c),
                                                              chk(st_f, c).bitcast(F32), wbc[:], ALU.add),
                               BONC)

                SMM = None
                for c in range(8):
                    SMM = em.t(lambda t, c=c: t.matmul(pls16, sel_sb[:], chk(st_f, c),
                                                       start=(c == 0), stop=(c == 7)),
                               XW2, INEV, SMM)
                SQMM = None
                SQS = None
                for c in range(8):
                    SQS = em.s(lambda s, c=c: s.activation(sq_f[:],
                                                           chk(st_f, c).bitcast(F32), AF.Square),
                               XW2, SQMM, BCMM)
                    SQMM = em.t(lambda t, c=c: t.matmul(pstB[:, 0:BL], sel_sb[:], sq_f[:],
                                                        start=(c == 0), stop=(c == 7)),
                                SQS, tk["pstB_free"])
                MU = em.v(lambda v: v.tensor_scalar_mul(sm0[:], pls16, 1.0 / 64), SMM, LNN)
                M2 = em.v(lambda v: v.tensor_tensor(sm1[:], sm0[:], sm0[:], ALU.mult), MU, INV)
                VAR = em.v(lambda v: v.scalar_tensor_tensor(sm1[:], pstB[:, 0:BL], 1.0 / 64, sm1[:],
                                                            ALU.mult, ALU.subtract), M2, SQMM)
                LNV = em.s(lambda s: s.activation(smI[:], sm1[:], AF.Ln, bias=eps16[:]), VAR, BCMM, EPS, D_IN)
                RSTD = em.s(lambda s: s.activation(smC[:], smI[:].bitcast(F32), AF.Exp, scale=-0.5), LNV, tk["smR_free"], BCMM)
                B0 = em.v(lambda v: v.tensor_tensor(sm1[:], sm0[:], smC[:].bitcast(F32), ALU.mult),
                          RSTD, VAR)
                BN = em.v(lambda v: v.tensor_scalar_mul(smB[:], sm1[:], -1.0), B0, tk["smB_free"])

                MIXG = mix("g", A1)
                G1 = em.t(lambda t: mml(t, 1024), MIXG, TA, CBC, KV2)
                TG = em.s(lambda s: s.activation(tl[:], pl64, AF.Tanh), G1, A2)

                XNGDEP = []
                UT = None
                AMM = em.t(lambda t: t.matmul(pb[:, 0:BL], E16_sb[:], smC[:], start=True, stop=True),
                           RSTD, BCMM, XW2)
                BMM = em.t(lambda t: t.matmul(PX[:, 0:BL], E16_sb[:], smB[:], start=True, stop=True),
                           BN, TG)
                for c in range(8):
                    U = em.v(lambda v, c=c: v.tensor_tensor(chk(st_f, c),
                                                            chk(st_f, c).bitcast(F32),
                                                            pb[:, 0:BL], ALU.mult), AMM, CARRY, SQMM if c == 0 else None)
                    U2 = em.v(lambda v, c=c: v.tensor_tensor(chk(st_f, c),
                                                             chk(st_f, c).bitcast(F32),
                                                             PX[:, 0:BL], ALU.add), U, BMM)
                    UT = em.v(lambda v, c=c: v.tensor_scalar(chk(st_f, c),
                                                             chk(st_f, c).bitcast(F32),
                                                             lnw_sb[:, c:c + 1], lnb_sb[:, c:c + 1],
                                                             ALU.mult, ALU.add), U2, WL)
                    G2 = em.t(lambda t, c=c: t.matmul(pj[c % 2][:, 0:BL],
                                                      l2_sb[:, 32 + c * 128:32 + (c + 1) * 128],
                                                      tl[:], start=True, stop=True),
                              TG, tk["evac"][c % 2])
                    GS = em.s(lambda s, c=c: s.activation(g_scr[:], pj[c % 2][:, 0:BL], AF.Sigmoid),
                              G2, tk["g_free"], tk["x1t_free"] if c == 0 else None)
                    tk["evac"][c % 2] = GS
                    XNG = em.v(lambda v, c=c: v.tensor_tensor(chk(xng, c), chk(st_f, c).bitcast(F32),
                                                              g_scr[:], ALU.mult), UT, GS, G1)
                    tk["g_free"] = XNG
                    XNGDEP.append(XNG)

                for q in range(BL // 128):
                    DMM = None
                    for n in range(2):
                        for c in range(8):
                            DMM = em.t(lambda t, q=q, n=n, c=c: t.matmul(
                                pd[:, n * 512:(n + 1) * 512],
                                chk(xng, c)[:, q * 128:(q + 1) * 128],
                                Wsb["o"][:, c * 1024 + n * 512:c * 1024 + (n + 1) * 512],
                                start=(c == 0), stop=(c == 7)),
                                XNGDEP[c], tk["pd_free"], DMM)
                    lastX2 = None
                    for n in range(4):
                        X1L = em.y(lambda yq, p=p, t0=t0, q=q, n=n: yq.dma_start(
                            out=x1t[:], in_=x1b[p, t0 + q * 128:t0 + (q + 1) * 128,
                                               n * 256:(n + 1) * 256]),
                            tk["x1t_free"], XNGDEP[-1])
                        X2A = em.v(lambda v, n=n: v.tensor_tensor(
                            x1t[:], x1t[:], pd[:, n * 256:(n + 1) * 256], ALU.add),
                            DMM, X1L, tk["store"], lastX2)
                        ST = em.g(lambda gq, p=p, t0=t0, q=q, n=n: gq.dma_start(
                            out=x2b[p, t0 + q * 128:t0 + (q + 1) * 128, n * 256:(n + 1) * 256],
                            in_=x1t[:]), X2A)
                        tk["x1t_free"] = ST
                        tk["store"] = ST
                        lastX2 = X2A
                    tk["pd_free"] = lastX2

                if dbg and p == 0 and j == 0:
                    D1 = dump("d_rf", r_f[:], EV_ALL[0], EV_ALL[1])
                    D2 = dump("d_kk", kk_f[:].bitcast(F32), EV_ALL[0], EV_ALL[1])
                    D3 = dump("d_vf", v_f[:], EV_ALL[0], EV_ALL[1])
                    D4 = dump("d_xng", st_f[:].bitcast(F32), XW2)
                    dump_sm(0, sm0[:], MU, M2)
                    dump_sm(1, smC[:].bitcast(F32), RSTD)
                    dump_sm(2, smW[:].bitcast(F32), WSM)
                    dump_sm(3, smC[:].bitcast(F32), CSM)
                    dump_sm(5, smW[:].bitcast(F32), BN)
                    tk["dbg"] = ("g", em.cnt["g"])

                tk["mix_free"] = DMM
                tk["sx_free"] = MIXG
                tk["zt_free"] = MIXG
                tk["sm_free"] = BN
                tk["PSA_free"] = MU
                tk["pstB_free"] = VAR
                tk["pb_free"] = UT
                tk["PX_free"] = UT
                tk["kkv_free"] = XNGDEP[-1]
                tk["smW_free"] = WBC
                tk["smB_free"] = BMM
                tk["smC_free"] = AMM
                tk["smR_free"] = None
                tk["PSA_free"] = MU

        em.barrier("g", [("g", em.cnt["g"]), ("y", em.cnt["y"])])
    return nc


def stage_consts(I):
    f32 = np.float32
    PERMQ = np.zeros((8, 128), np.int64)
    for q in range(C):
        h, d = q // 64, q % 64
        PERMQ[d // 8, h * 8 + d % 8] = q
    pf = PERMQ.reshape(-1)

    def pcols(W):
        return np.ascontiguousarray(W[:, pf])

    def prows(W):
        return np.ascontiguousarray(W[pf, :])

    def fvec(v):
        return v[pf].reshape(8, 128)

    out = {}
    out["Wr_p"] = pcols(I["L0_attn_Wr"].astype(f32))
    out["Wk_p"] = pcols(I["L0_attn_Wk"].astype(f32))
    out["Wv_p"] = pcols(I["L0_attn_Wv"].astype(f32))
    out["Wo_p"] = prows(I["L0_attn_Wo"].astype(f32))
    out["w1"] = I["L0_attn_w1"].astype(f32)
    out["a1"] = I["L0_attn_a1"].astype(f32)
    out["g1"] = I["L0_attn_g1"].astype(f32)
    out["w2"] = I["L0_attn_w2"].astype(f32)
    out["a2"] = I["L0_attn_a2"].astype(f32)
    out["g2_p"] = pcols(I["L0_attn_g2"].astype(f32))
    mus = np.zeros((128, 48), f32)
    for i, nm in enumerate(["r", "w", "k", "v", "a", "g"]):
        v = I[f"L0_attn_x_{nm}"][0, 0].astype(f32)
        mus[:, i * 8:(i + 1) * 8] = v.reshape(8, 128).T
    out["mus"] = mus
    out["kkf"] = np.ascontiguousarray(fvec(I["L0_attn_k_k"][0, 0].astype(f32)).T)
    out["w0h"] = I["L0_attn_w0"][0, 0, :, 0].astype(f32).reshape(16, 1)
    out["a0h"] = I["L0_attn_a0"][0, 0, :, 0].astype(f32).reshape(16, 1)
    E16 = np.zeros((16, 128), f32)
    for pp in range(128):
        E16[pp // 8, pp] = 1.0
    out["E16p"] = E16
    out["E16f"] = E16.copy()
    out["sel16"] = np.ascontiguousarray(E16.T)
    rk = I["L0_attn_r_k"].astype(f32)
    rke = np.zeros((128, 8 * 16), f32)
    for c in range(8):
        for pp in range(128):
            h, d = pp // 8, c * 8 + pp % 8
            rke[pp, c * 16 + h] = rk[h, d]
    out["rke"] = rke
    lnw, lnb = fvec(I["L0_attn_ln_w"].astype(f32)), fvec(I["L0_attn_ln_b"].astype(f32))
    out["lnwf"] = np.ascontiguousarray(lnw.T)
    out["lnbf"] = np.ascontiguousarray(lnb.T)
    return out


def stage_core_inputs(core, x1, z1, consts):
    b0 = core * BSH
    x1b = np.ascontiguousarray(x1[b0:b0 + BSH].astype(np.float32))
    z1t = np.zeros((BSH, C, T + 1), np.float32)
    z1t[:, :, 1:] = z1[b0:b0 + BSH].astype(np.float32).transpose(0, 2, 1)
    m = {"x1b": x1b, "z1tb": z1t}
    m.update(consts)
    return m




B, SSH, C, H, D = 16, 512, 1024, 16, 64
NT = 64
SCALE = 0.125



def emit_ln_stats(em, b6, mv, sc4, sc1a, sc1b, x_ap_a, x_ap_b, deps, mv_free=None):
    V = mybir.AxisListType.X
    BN1 = em.v(lambda v: v.bn_stats(b6[:, 0:6], x_ap_a), *deps)
    BN2 = em.v(lambda v: v.bn_stats(b6[:, 6:12], x_ap_b), BN1)
    mview = b6[:].rearrange("p (g k) -> p g k", k=3)[:, :, 1]
    m2view = b6[:].rearrange("p (g k) -> p g k", k=3)[:, :, 2]
    S1 = em.v(lambda v: v.reduce_sum(sc1a[:], mview, axis=V), BN2)
    S2 = em.v(lambda v: v.reduce_sum(sc1b[:], m2view, axis=V), S1)
    MEAN = em.v(lambda v: v.tensor_scalar_mul(mv[:, 0:1], sc1a[:], 0.25), S2, mv_free)
    P = em.v(lambda v: v.tensor_tensor(sc4[:], mview, mview, ALU.mult), MEAN)
    S3 = em.v(lambda v: v.reduce_sum(sc1a[:], sc4[:], axis=V), P)
    V1 = em.v(lambda v: v.tensor_scalar_mul(sc1b[:], sc1b[:], 1.0 / 1024.0), S3)
    V2 = em.v(lambda v: v.scalar_tensor_tensor(sc1b[:], sc1a[:], 0.25, sc1b[:], ALU.mult, ALU.add), V1)
    M2 = em.v(lambda v: v.tensor_tensor(sc1a[:], mv[:, 0:1], mv[:, 0:1], ALU.mult), V2)
    VAR = em.v(lambda v: v.tensor_tensor(mv[:, 1:2], sc1b[:], sc1a[:], ALU.subtract), M2)
    return VAR


def build_pa():
    ctx = contextlib.ExitStack()
    nc = bass.Bass("TRN2", target_bir_lowering=False)
    ap = lambda name, shp, dt, k="ExternalInput": nc.dram_tensor(name, shp, dt, kind=k).ap()
    xs = ap("xs", [B, SSH, C], F32)
    xti = ap("xti", [C, B * SSH], F32R)
    Wgi = ap("Wg", [C, H], F32R)
    Wqi = ap("Wq", [H, C, D], F32)
    Wkvi = ap("Wkv", [H, C, 2 * D], F32)
    bqi = ap("bqT", [D, H], F32)
    bki = ap("bkT", [D, H], F32)
    bvi = ap("bvT", [D, H], F32)
    bdi = ap("bdmask", [128, 128], BF16)
    idfi = ap("idf", [128, 128], F32)
    idbi = ap("idb", [128, 128], BF16)
    G_o = ap("G_o", [NT, 128, 128], BF16, "ExternalOutput")
    Gt_o = ap("Gt_o", [NT, 128, 128], BF16, "ExternalOutput")
    v1_o = ap("v1_o", [D, H * SSH], F32, "ExternalOutput")
    att_o = ap("att_o", [1, H * SSH], F32, "ExternalOutput")
    st_o = ap("st_o", [NT, 128, 2], F32, "ExternalOutput")
    zm_o = ap("zm_o", [1, 2 * H], F32, "ExternalOutput")

    sbuf = lambda name, shp, dt: ctx.enter_context(nc.sbuf_tensor(name, shp, dt))

    Wq_sb = sbuf("Wq_sb", [128, H * 8 * D], BF16)
    Wkv_sb = sbuf("Wkv_sb", [128, H * 8 * 2 * D], BF16)
    Wg_sb = sbuf("Wg_sb", [128, 8 * H], F32R)
    bq_sb = sbuf("bq_sb", [64, H], F32)
    bk_sb = sbuf("bk_sb", [64, H], F32)
    bv_sb = sbuf("bv_sb", [64, H], F32)
    bd_sb = sbuf("bd_sb", [128, 128], BF16)
    idf_sb = sbuf("idf_sb", [128, 128], F32)
    idb_sb = sbuf("idb_sb", [128, 128], BF16)
    ones1 = sbuf("ones1", [128, 1], F32R)
    onesr = sbuf("onesr", [1, 128], F32R)
    cs_row = sbuf("cs_row", [1, H], F32R)
    epsA = sbuf("epsA", [128, 1], F32)
    xt = sbuf("xt", [128, 2 * C], F32)
    xtc = sbuf("xtc", [128, 2 * 1024], F32R)
    z0b = sbuf("z0b", [128, 2 * C], BF16)
    bn6 = sbuf("bn6", [128, 12], F32)
    mv = sbuf("mv", [128, 2], F32)
    sc4s = sbuf("sc4s", [128, 4], F32)
    sc1a = sbuf("sc1a", [128, 1], F32)
    sc1b = sbuf("sc1b", [128, 1], F32)
    stt_ = sbuf("stt_", [128, 2], F32)
    nmr = sbuf("nmr", [128, 1], F32)
    rstd = sbuf("rstd", [128, 1], F32)
    sqv = sbuf("sqv", [128, 1], F32)
    csb = sbuf("csb", [128, H], F32)
    sc0 = sbuf("sc0", [128, H], F32)
    scr = sbuf("scr", [128, H], F32)
    u1 = sbuf("u1", [128, H], F32)
    mx = sbuf("mx", [128, 1], F32)
    lse = sbuf("lse", [128, 1], F32)
    eh = sbuf("eh", [128, H], F32)
    scT = sbuf("scT", [16, 128], F32)
    mxT = sbuf("mxT", [16, 8], F32)
    eqT = sbuf("eqT", [16, 128], BF16)
    eq = sbuf("eq", [128, H], BF16)
    G_t = sbuf("G_t", [128, 2 * 128], BF16)
    Gt_t = sbuf("Gt_t", [128, 2 * 128], BF16)
    hgT = sbuf("hgT", [128, 8 * 2048], BF16)
    qb = sbuf("qb", [64, 128], F32)
    qkr = sbuf("qkr", [64, 128], F32R)
    v1_sb = sbuf("v1_sb", [64, H * SSH], F32)
    att_sb = sbuf("att_sb", [1, H * SSH], F32)
    zm_sb = sbuf("zm_sb", [1, 2 * H], F32)
    exs = sbuf("exs", [1, SSH], F32)

    psum = lambda name, dt=F32: ctx.enter_context(nc.psum_tensor(name, [128, 512], dt))
    pL = psum("pL")
    pT = psum("pT")
    pTb = psum("pTb", BF16)
    pG = psum("pG")
    pQ = psum("pQ")
    pK = psum("pK")
    pA = psum("pA")
    pC = psum("pC")

    with ctx:
        block = ctx.enter_context(nc.Block())
        em = Em(nc, block, ctx)

        def wload(g):
            for h in range(H):
                for c in range(8):
                    g.dma_start(out=Wq_sb[:, (h * 8 + c) * D:(h * 8 + c + 1) * D],
                                in_=Wqi[h, c * 128:(c + 1) * 128, :])
                    g.dma_start(out=Wkv_sb[:, (h * 8 + c) * 2 * D:(h * 8 + c + 1) * 2 * D],
                                in_=Wkvi[h, c * 128:(c + 1) * 128, :])
            for c in range(8):
                g.dma_start(out=Wg_sb[:, c * H:(c + 1) * H], in_=Wgi[c * 128:(c + 1) * 128, :])
            g.dma_start(out=bq_sb[:], in_=bqi)
            g.dma_start(out=bk_sb[:], in_=bki)
            g.dma_start(out=bv_sb[:], in_=bvi)
            g.dma_start(out=bd_sb[:], in_=bdi)
            g.dma_start(out=idf_sb[:], in_=idfi)
            g.dma_start(out=idb_sb[:], in_=idbi)
        WL = em.g(wload)
        ON = em.v(lambda v: v.memset(ones1[:].bitcast(F32), 1.0))
        ON2 = em.v(lambda v: v.memset(onesr[:].bitcast(F32), 1.0))
        EPSI = em.v(lambda v: v.memset(epsA[:], 1e-5))
        CS = None
        for c in range(8):
            CS = em.t(lambda t, c=c: t.matmul(pC[0:1, 0:H], ones1[:, 0:1], Wg_sb[:, c * H:(c + 1) * H],
                                              start=(c == 0), stop=(c == 7)), WL, ON, CS)
        CSE = em.v(lambda v: v.tensor_copy(cs_row[:], pC[0:1, 0:H]), CS)
        CSB0 = em.t(lambda t: t.matmul(pC[:, 64:64 + H], onesr[:], cs_row[:], start=True, stop=True),
                    CSE, ON2)
        CSB = em.s(lambda s: s.copy(csb[:], pC[:, 64:64 + H]), CSB0)

        tkA = {"xt": [None, None], "xtc": [None, None], "z0b": [None, None],
               "G": [None, None], "Gt": [None, None],
               "pL": None, "pT": None, "pTb": None, "pG": None,
               "pQ": None, "pK": None, "pA": None,
               "hg_rd": None, "sc_free": CSB, "eq_free": None, "st_free": None}

        vatt_toks = []
        for i in range(NT):
            bi = i % 2
            isb = i % 16
            sb = i // 16

            XL = em.y(lambda y_, i=i, bi=bi: y_.dma_start(
                out=xt[:, bi * C:(bi + 1) * C],
                in_=xs[:, 8 * i:8 * i + 8, :].transpose([1, 0, 2])), tkA["xt"][bi])

            def xcl(y_, i=i, bi=bi):
                for c in range(8):
                    y_.dma_start(out=xtc[:, bi * 1024 + c * 128:bi * 1024 + (c + 1) * 128],
                                 in_=xti[c * 128:(c + 1) * 128, 128 * i:128 * (i + 1)])
            XC = em.y(xcl, tkA["xtc"][bi])

            AGG = emit_ln_stats(em, bn6, mv, sc4s, sc1a, sc1b,
                                xt[:, bi * C:bi * C + 512], xt[:, bi * C + 512:bi * C + 1024],
                                (XL, tkA["st_free"]), mv_free=tkA.get("mv_free"))
            SQV = em.s(lambda s: s.activation(sqv[:], mv[:, 1:2], AF.Sqrt, bias=epsA[:]), AGG, EPSI)
            tkA["mv_free"] = SQV
            RCP = em.v(lambda v: v.reciprocal(rstd[:], sqv[:]), SQV)
            NMR = em.v(lambda v: v.scalar_tensor_tensor(nmr[:], mv[:, 0:1], -1.0, rstd[:],
                                                        ALU.mult, ALU.mult), RCP)
            STS = em.v(lambda v: v.tensor_copy(stt_[:, 0:1], mv[:, 0:1]), AGG)
            STS2 = em.v(lambda v: v.tensor_copy(stt_[:, 1:2], rstd[:]), RCP, STS)
            STO = em.g(lambda g, i=i: g.dma_start(out=st_o[i], in_=stt_[:]), STS2)
            tkA["st_free"] = STO
            Z0 = em.s(lambda s, bi=bi: s.activation(z0b[:, bi * C:(bi + 1) * C], xt[:, bi * C:(bi + 1) * C],
                                                    AF.Identity, bias=nmr[:], scale=rstd[:]),
                      NMR, tkA["z0b"][bi])

            LMM = None
            for c in range(8):
                LMM = em.t(lambda t, c=c, bi=bi: t.matmul(
                    pL[:, 0:H], xtc[:, bi * 1024 + c * 128:bi * 1024 + (c + 1) * 128],
                    Wg_sb[:, c * H:(c + 1) * H], start=(c == 0), stop=(c == 7)),
                    XC, WL, tkA["pL"], LMM)
            U1 = em.v(lambda v: v.tensor_scalar_mul(u1[:], csb[:], nmr[:]), NMR, CSB, LMM)
            SC0 = em.v(lambda v: v.scalar_tensor_tensor(sc0[:], pL[:, 0:H], rstd[:], u1[:],
                                                        ALU.mult, ALU.add), U1, tkA["sc_free"])
            tkA["pL"] = SC0
            MXx = em.v(lambda v: v.reduce_max(mx[:], sc0[:], axis=mybir.AxisListType.X, negate=True), SC0)
            EH = em.s(lambda s: s.activation(eh[:], sc0[:], AF.Exp, bias=mx[:]), MXx)
            LSE = em.v(lambda v: v.reduce_sum(lse[:], eh[:], axis=mybir.AxisListType.X), EH)
            LLN = em.s(lambda s: s.activation(lse[:], lse[:], AF.Ln), LSE)
            C1 = em.v(lambda v: v.tensor_sub(mx[:], mx[:], lse[:]), LLN)
            SCP = em.v(lambda v: v.tensor_scalar_add(scr[:], sc0[:], mx[:]), C1)
            tkA["sc_free"] = MXx

            TSC = em.t(lambda t: t.transpose(pT[0:16, 0:128], scr[:, 0:H], idf_sb[:]), SCP, tkA["pT"])
            TSE = em.v(lambda v: v.tensor_copy(scT[:], pT[0:16, 0:128]), TSC)
            tkA["pT"] = TSE
            SMX = em.v(lambda v: v.reduce_max(mxT[:], scT[:].rearrange("p (s b) -> p s b", b=16),
                                              axis=mybir.AxisListType.X), TSE)
            EQT = em.v(lambda v: v.tensor_tensor(
                eqT[:].rearrange("p (s b) -> p s b", b=16),
                scT[:].rearrange("p (s b) -> p s b", b=16),
                mxT[:].unsqueeze(2).broadcast_to([16, 8, 16]), ALU.is_equal),
                SMX, tkA["eq_free"])
            TEQ = em.t(lambda t: t.transpose(pTb[0:128, 0:16], eqT[:], idb_sb[0:16, 0:16]),
                       EQT, tkA["pTb"])
            EQE = em.v(lambda v: v.tensor_copy(eq[:], pTb[0:128, 0:16]), TEQ)
            tkA["eq_free"] = EQE
            GB = em.v(lambda v, bi=bi: v.tensor_tensor(
                G_t[:, bi * 128:(bi + 1) * 128].rearrange("p (h j) -> p h j", h=16),
                bd_sb[:].rearrange("p (h j) -> p h j", h=16),
                eq[:].unsqueeze(2).broadcast_to([128, 16, 8]), ALU.mult),
                EQE, tkA["G"][bi])
            TGT = em.t(lambda t, bi=bi: t.transpose(pTb[:, 128:256], G_t[:, bi * 128:(bi + 1) * 128],
                                                    idb_sb[:]), GB, EQE, tkA["pTb"])
            GTE = em.v(lambda v, bi=bi: v.tensor_copy(Gt_t[:, bi * 128:(bi + 1) * 128], pTb[:, 128:256]),
                       TGT, tkA["Gt"][bi])
            tkA["pTb"] = GTE
            GO = em.g(lambda g, i=i, bi=bi: g.dma_start(out=G_o[i], in_=G_t[:, bi * 128:(bi + 1) * 128]), GB)
            GTO = em.g(lambda g, i=i, bi=bi: g.dma_start(out=Gt_o[i], in_=Gt_t[:, bi * 128:(bi + 1) * 128]), GTE)
            tkA["G"][bi] = GO
            tkA["Gt"][bi] = GTO

            GMM = None
            HGE = tkA["pG"]
            for c in range(8):
                GMM = em.t(lambda t, c=c, bi=bi: t.matmul(
                    pG[:, 0:128], z0b[:, bi * C + c * 128:bi * C + (c + 1) * 128],
                    G_t[:, bi * 128:(bi + 1) * 128], start=True, stop=True),
                    GB, Z0, HGE, tkA["hg_rd"] if (isb == 0 and c == 0) else None, GMM)
                HGE = em.s(lambda s, c=c, isb=isb: s.activation(
                    hgT[:, c * 2048:(c + 1) * 2048].rearrange("p (hh n) -> p hh n", hh=16)[:, :, isb * 8:isb * 8 + 8],
                    pG[:, 0:128].rearrange("p (hh j) -> p hh j", hh=16), AF.Copy), GMM)
            tkA["pG"] = HGE
            tkA["z0b"][bi] = GMM
            tkA["xt"][bi] = Z0
            tkA["xtc"][bi] = LMM

            if isb == 15:
                s0 = sb * 128
                prevQK, prevVEV, prevAEV = None, None, None
                for h in range(H):
                    QMM = None
                    for c in range(8):
                        QMM = em.t(lambda t, h=h, c=c: t.matmul(
                            pQ[0:64, 0:128], Wq_sb[:, (h * 8 + c) * D:(h * 8 + c + 1) * D],
                            hgT[:, c * 2048 + h * 128:c * 2048 + (h + 1) * 128],
                            start=(c == 0), stop=(c == 7)),
                            HGE, WL, tkA["pQ"] if c == 0 else None, QMM)
                    QEV = em.s(lambda s, h=h: s.activation(qb[:], pQ[0:64, 0:128], AF.Identity,
                                                           bias=bq_sb[:, h:h + 1]), QMM)
                    tkA["pQ"] = QEV
                    KMM = None
                    for c in range(8):
                        KMM = em.t(lambda t, h=h, c=c: t.matmul(
                            pK[0:128, 0:128], Wkv_sb[:, (h * 8 + c) * 2 * D:(h * 8 + c + 1) * 2 * D],
                            hgT[:, c * 2048 + h * 128:c * 2048 + (h + 1) * 128],
                            start=(c == 0), stop=(c == 7)),
                            HGE, (prevQK if c == 0 else None), (prevVEV if c == 0 else None), KMM)
                    QK = em.v(lambda v, h=h: v.scalar_tensor_tensor(qkr[:], pK[0:64, 0:128],
                                                                    bk_sb[:, h:h + 1], qb[:],
                                                                    ALU.add, ALU.mult), KMM, QEV, prevAEV)
                    VEV = em.s(lambda s, h=h, s0=s0: s.activation(
                        v1_sb[:, h * SSH + s0:h * SSH + s0 + 128], pK[64:128, 0:128], AF.Identity,
                        bias=bv_sb[:, h:h + 1]), KMM)
                    AMM = em.t(lambda t: t.matmul(pA[0:1, 0:128], ones1[0:64, 0:1], qkr[:],
                                                  start=True, stop=True), QK, tkA["pA"], ON)
                    AEV = em.s(lambda s, h=h, s0=s0: s.activation(
                        att_sb[0:1, h * SSH + s0:h * SSH + s0 + 128], pA[0:1, 0:128], AF.Copy,
                        scale=SCALE), AMM)
                    tkA["pA"] = AEV
                    prevQK, prevVEV, prevAEV = QK, VEV, AEV
                    vatt_toks.append(AEV)
                    vatt_toks.append(VEV)
                tkA["hg_rd"] = KMM

        MXA = em.v(lambda v: v.reduce_max(zm_sb[0:1, 0:H], att_sb[:].rearrange("p (h s) -> p h s", h=H),
                                          axis=mybir.AxisListType.X, negate=True),
                   vatt_toks[-1], vatt_toks[-2])
        SUME = None
        for h in range(H):
            EX = em.s(lambda s, h=h: s.activation(exs[:], att_sb[0:1, h * SSH:(h + 1) * SSH], AF.Exp,
                                                  bias=zm_sb[0:1, h:h + 1]), MXA, SUME)
            SUME = em.v(lambda v, h=h: v.reduce_sum(zm_sb[0:1, H + h:H + h + 1], exs[:],
                                                    axis=mybir.AxisListType.X), EX)
        VO = em.g(lambda g: g.dma_start(out=v1_o, in_=v1_sb[:]), *vatt_toks)
        AO = em.g(lambda g: g.dma_start(out=att_o, in_=att_sb[:]), SUME)
        ZMO = em.g(lambda g: g.dma_start(out=zm_o, in_=zm_sb[:]), SUME)
        em.barrier("g", [("g", em.cnt["g"]), ("y", em.cnt["y"])])
    return nc


def build_pb(emit_z=True, tile_barrier=False, serial=True):
    ctx = contextlib.ExitStack()
    nc = bass.Bass("TRN2", target_bir_lowering=False)
    ap = lambda name, shp, dt, k="ExternalInput": nc.dram_tensor(name, shp, dt, kind=k).ap()
    xs = ap("xs", [B, SSH, C], F32)
    Gt_i = ap("Gt_i", [NT, 128, 128], BF16)
    v1_i = ap("v1_i", [D, H * SSH], F32)
    att_i = ap("att_i", [1, H * SSH], F32R)
    st_i = ap("st_i", [NT, 128, 2], F32)
    zmM = ap("zmM", [H, 8], F32)
    zmS = ap("zmS", [H, 8], F32)
    WoFi = ap("WoF", [H, D, C], F32)
    boi = ap("bo", [1, C], F32R)
    lngi = ap("lng", [1, C], F32R)
    lnbi = ap("lnb2", [1, C], F32R)
    phseli = ap("phsel", [H, H * D], F32R)
    x1_o = ap("x1_o", [B, SSH, C], F32, "ExternalOutput")
    z1_o = ap("z1_o", [B, SSH, C], F32, "ExternalOutput") if emit_z else None
    aoD = nc.dram_tensor("aoD", [4, 128, H * C], BF16).ap()

    sbuf = lambda name, shp, dt: ctx.enter_context(nc.sbuf_tensor(name, shp, dt))
    Wo_sb = sbuf("Wo_sb", [64, H * C], BF16)
    bo_bc = sbuf("bo_bc", [128, C], F32)
    gh_bc = sbuf("gh_bc", [128, C], F32)
    bh_bc = sbuf("bh_bc", [128, C], F32)
    onesr = sbuf("onesr", [1, 128], F32R)
    halfr = sbuf("halfr", [1, 128], F32R)
    epsA = sbuf("epsA", [128, 1], F32)
    ph_sb = sbuf("ph_sb", [16, H * D], F32R)
    row_sb = sbuf("row_sb", [1, 3 * C], F32R)
    v1_sb = sbuf("v1_sb", [64, H * SSH], F32)
    p_sb = sbuf("p_sb", [16, SSH], F32R)
    zmt = sbuf("zmt", [16, 24], F32)
    mg = sbuf("mg", [16, 4], F32)
    av_sb = sbuf("av_sb", [64, 2 * SSH], BF16)
    ao_ch = sbuf("ao_ch", [128, C], BF16)
    aoI = sbuf("aoI", [128, 2 * C], BF16)
    Gt_t = sbuf("Gt_t", [128, 2 * 128], BF16)
    xt = sbuf("xt", [128, 2 * C], F32)
    stt_ = sbuf("stt_", [128, 2 * 2], F32)
    z0f = sbuf("z0f", [128, C], F32)
    y_f = sbuf("y_f", [128, C], F32)
    bn6 = sbuf("bn6", [128, 12], F32)
    mv = sbuf("mv", [128, 2], F32)
    sc4s = sbuf("sc4s", [128, 4], F32)
    sc1a = sbuf("sc1a", [128, 1], F32)
    sc1b = sbuf("sc1b", [128, 1], F32)
    sqv = sbuf("sqv", [128, 1], F32)
    rstd2 = sbuf("rstd2", [128, 1], F32)
    nmr2 = sbuf("nmr2", [128, 1], F32)
    nmr0 = sbuf("nmr0", [128, 1], F32)
    zout = sbuf("zout", [128, C], F32)

    pAO = ctx.enter_context(nc.psum_tensor("pAO", [128, 1024], F32))
    pOU = ctx.enter_context(nc.psum_tensor("pOU", [128, 1024], F32))
    pP = ctx.enter_context(nc.psum_tensor("pP", [128, 512], F32))
    pB2 = ctx.enter_context(nc.psum_tensor("pB2", [128, 512], F32))

    with ctx:
        block = ctx.enter_context(nc.Block())
        em = Em(nc, block, ctx, serial=serial)

        def wload(g):
            for h in range(H):
                g.dma_start(out=Wo_sb[:, h * C:(h + 1) * C], in_=WoFi[h])
            g.dma_start(out=v1_sb[:], in_=v1_i)
            g.dma_start(out=zmt[:, 0:8], in_=zmM)
            g.dma_start(out=zmt[:, 8:16], in_=zmS)
            g.dma_start(out=ph_sb[:], in_=phseli)
            g.dma_start(out=p_sb[:], in_=att_i.rearrange("o (h s) -> (o h) s", h=H))
            g.dma_start(out=row_sb[0:1, 0:C], in_=boi)
            g.dma_start(out=row_sb[0:1, C:2 * C], in_=lngi)
            g.dma_start(out=row_sb[0:1, 2 * C:3 * C], in_=lnbi)
        WL = em.g(wload)
        ON2 = em.v(lambda v: v.memset(onesr[:].bitcast(F32), 1.0))
        HF = em.v(lambda v: v.memset(halfr[:].bitcast(F32), 0.5))
        EPSI = em.v(lambda v: v.memset(epsA[:], 1e-5))

        prev = None
        for gi, (dst, lz) in enumerate([(bo_bc, onesr), (gh_bc, halfr), (bh_bc, halfr)]):
            for n in range(2):
                M1 = em.t(lambda t, gi=gi, lz=lz, n=n: t.matmul(
                    pB2[:, 0:512], lz[:], row_sb[0:1, gi * C + n * 512:gi * C + (n + 1) * 512],
                    start=True, stop=True), ON2, HF, WL, prev)
                prev = em.s(lambda s, dst=dst, n=n: s.copy(dst[:, n * 512:(n + 1) * 512], pB2[:, 0:512]), M1)
        BCD = prev

        GM2 = em.v(lambda v: v.tensor_reduce(mg[:, 0:1], zmt[:, 0:8], axis=mybir.AxisListType.X,
                                             op=ALU.min), WL)
        ECd = em.v(lambda v: v.scalar_tensor_tensor(zmt[:, 16:24], zmt[:, 0:8], -1.0,
                                                    mg[:, 0:1].broadcast_to([16, 8]),
                                                    ALU.mult, ALU.add), GM2)
        EC = em.s(lambda s: s.activation(zmt[:, 16:24], zmt[:, 16:24], AF.Exp), ECd)
        ZZ = em.v(lambda v: v.tensor_tensor(zmt[:, 16:24], zmt[:, 16:24], zmt[:, 8:16], ALU.mult), EC)
        ZG = em.v(lambda v: v.reduce_sum(mg[:, 1:2], zmt[:, 16:24], axis=mybir.AxisListType.X), ZZ)
        RZ = em.v(lambda v: v.reciprocal(mg[:, 2:3], mg[:, 1:2]), ZG)
        PE1 = em.s(lambda s: s.activation(p_sb[:], p_sb[:].bitcast(F32), AF.Exp,
                                          bias=mg[:, 0:1]), WL, GM2)
        PE2 = em.v(lambda v: v.tensor_scalar_mul(p_sb[:], p_sb[:].bitcast(F32), mg[:, 2:3]), PE1, RZ)

        prevAV = [None, None]
        prevPBC = None
        prevAOS = {}
        tk_pAO = None
        for h in range(H):
            bi = h % 2
            PBC = em.t(lambda t, h=h: t.matmul(pP[0:64, 0:SSH], ph_sb[:, h * D:(h + 1) * D],
                                               p_sb[:], start=True, stop=True),
                       PE2, WL, BCD, prevPBC)
            AV = em.v(lambda v, h=h, bi=bi: v.tensor_tensor(
                av_sb[:, bi * SSH:(bi + 1) * SSH], v1_sb[:, h * SSH:(h + 1) * SSH],
                pP[0:64, 0:SSH], ALU.mult), PBC, WL, prevAV[bi])
            prevPBC = AV
            lastAOM = None
            for q in range(4):
                AOM = None
                for n in range(2):
                    AOM = em.t(lambda t, h=h, q=q, n=n, bi=bi: t.matmul(
                        pAO[:, n * 512:(n + 1) * 512],
                        av_sb[:, bi * SSH + q * 128:bi * SSH + q * 128 + 128],
                        Wo_sb[:, h * C + n * 512:h * C + (n + 1) * 512],
                        start=True, stop=True), AV, tk_pAO if n == 0 else None, AOM)
                AOE = em.v(lambda v, h=h, q=q: v.tensor_tensor(
                    ao_ch[:], pAO[:, 0:1024], bo_bc[:], ALU.add),
                    AOM, BCD, prevAOS.get(0))
                tk_pAO = AOE
                AOS = em.g(lambda g, h=h, q=q: g.dma_start(
                    out=aoD[q, :, h * C:(h + 1) * C], in_=ao_ch[:]), AOE)
                prevAOS[0] = AOS
                lastAOM = AOM
            prevAV[bi] = lastAOM
        AOALL = prevAOS[0]

        tkB = {"aoI": [None, None], "Gt": [None, None], "xt": [None, None],
               "pOU": None, "y": None, "zo": None}
        STL = [None, None]

        for i in range(NT):
            bi = i % 2
            q = i // 16
            if tile_barrier:
                snap = [(e, em.cnt[e]) for e in ("t", "v", "s", "y", "g")]
                for e in ("t", "v", "s", "y", "g"):
                    em.barrier(e, snap)
            XL = em.y(lambda y_, i=i, bi=bi: y_.dma_start(
                out=xt[:, bi * C:(bi + 1) * C],
                in_=xs[:, 8 * i:8 * i + 8, :].transpose([1, 0, 2])), tkB["xt"][bi])
            STLD = em.y(lambda y_, i=i, bi=bi: y_.dma_start(
                out=stt_[:, bi * 2:bi * 2 + 2], in_=st_i[i]), STL[bi])
            GTL = em.y(lambda y_, i=i, bi=bi: y_.dma_start(
                out=Gt_t[:, bi * 128:(bi + 1) * 128], in_=Gt_i[i]), tkB["Gt"][bi])
            AIL = em.y(lambda y_, i=i, q=q, bi=bi: y_.dma_start(
                out=aoI[:, bi * C:(bi + 1) * C],
                in_=aoD[q].rearrange("r (h d) -> r h d", h=H)[8 * (i % 16):8 * (i % 16) + 8]
                    .transpose([1, 0, 2])), tkB["aoI"][bi], AOALL)
            SCM = None
            for n in range(2):
                SCM = em.t(lambda t, n=n, bi=bi: t.matmul(
                    pOU[:, n * 512:(n + 1) * 512], Gt_t[:, bi * 128:(bi + 1) * 128],
                    aoI[:, bi * C + n * 512:bi * C + (n + 1) * 512],
                    start=True, stop=True), GTL, AIL, tkB["pOU"] if n == 0 else None, SCM)
            NM2 = em.v(lambda v, bi=bi: v.scalar_tensor_tensor(nmr0[:], stt_[:, bi * 2:bi * 2 + 1], -1.0,
                                                               stt_[:, bi * 2 + 1:bi * 2 + 2],
                                                               ALU.mult, ALU.mult), STLD)
            Z0F = em.s(lambda s, bi=bi: s.activation(z0f[:], xt[:, bi * C:(bi + 1) * C], AF.Identity,
                                                     scale=stt_[:, bi * 2 + 1:bi * 2 + 2],
                                                     bias=nmr0[:]), NM2, XL, tkB["y"])
            YT = em.v(lambda v: v.tensor_tensor(y_f[:], z0f[:], pOU[:, 0:1024], ALU.add), Z0F, SCM)
            tkB["pOU"] = YT
            AGG = emit_ln_stats(em, bn6, mv, sc4s, sc1a, sc1b,
                                y_f[:, 0:512], y_f[:, 512:1024], (YT,), mv_free=tkB.get("mv_free"))
            SQV = em.s(lambda s: s.activation(sqv[:], mv[:, 1:2], AF.Sqrt, bias=epsA[:]), AGG, EPSI)
            tkB["mv_free"] = SQV
            RCP = em.v(lambda v: v.reciprocal(rstd2[:], sqv[:]), SQV)
            NMR = em.v(lambda v: v.scalar_tensor_tensor(nmr2[:], mv[:, 0:1], -1.0, rstd2[:],
                                                        ALU.mult, ALU.mult), RCP)
            T2 = em.s(lambda s: s.activation(z0f[:], y_f[:], AF.Identity, scale=rstd2[:], bias=nmr2[:]),
                      NMR, YT)
            U = em.v(lambda v: v.tensor_tensor(y_f[:], z0f[:], gh_bc[:], ALU.mult), T2, BCD)
            X1A = em.v(lambda v, bi=bi: v.tensor_tensor(xt[:, bi * C:(bi + 1) * C],
                                                        xt[:, bi * C:(bi + 1) * C], y_f[:], ALU.add),
                       U, Z0F)
            X1B = em.v(lambda v, bi=bi: v.tensor_tensor(xt[:, bi * C:(bi + 1) * C],
                                                        xt[:, bi * C:(bi + 1) * C], bh_bc[:], ALU.add),
                       X1A, BCD)
            XO = em.g(lambda g, i=i, bi=bi: g.dma_start(
                out=x1_o[:, 8 * i:8 * i + 8, :].transpose([1, 0, 2]),
                in_=xt[:, bi * C:(bi + 1) * C]), X1B)
            tkB["y"] = X1B
            STL[bi] = NM2
            tkB["Gt"][bi] = SCM
            tkB["aoI"][bi] = SCM
            tkB["xt"][bi] = XO
            if emit_z:
                AGGz = emit_ln_stats(em, bn6, mv, sc4s, sc1a, sc1b,
                                     xt[:, bi * C:bi * C + 512], xt[:, bi * C + 512:bi * C + 1024],
                                     (X1B,), mv_free=tkB.get("mv_free"))
                SQVz = em.s(lambda s: s.activation(sqv[:], mv[:, 1:2], AF.Sqrt, bias=epsA[:]), AGGz, T2)
                tkB["mv_free"] = SQVz
                RCPz = em.v(lambda v: v.reciprocal(rstd2[:], sqv[:]), SQVz)
                NMRz = em.v(lambda v: v.scalar_tensor_tensor(nmr2[:], mv[:, 0:1], -1.0, rstd2[:],
                                                             ALU.mult, ALU.mult), RCPz)
                ZA = em.s(lambda s, bi=bi: s.activation(zout[:], xt[:, bi * C:(bi + 1) * C], AF.Identity,
                                                        scale=rstd2[:], bias=nmr2[:]),
                          NMRz, tkB["zo"])
                ZO = em.g(lambda g, i=i: g.dma_start(
                    out=z1_o[:, 8 * i:8 * i + 8, :].transpose([1, 0, 2]), in_=zout[:]), ZA)
                tkB["zo"] = ZO
                tkB["y"] = ZA
                STL[bi] = NMRz
                tkB["xt"][bi] = ZO
        em.barrier("g", [("g", em.cnt["g"]), ("y", em.cnt["y"])])
    return nc


NCORES = 8
TFULL = 4096

_progs = {}

def _get(name, fn):
    if name not in _progs:
        _progs[name] = fn()
    return _progs[name]


def _stage_pa_consts(I):
    f32 = np.float32
    BFD = ml_dtypes.bfloat16
    bq = I["L0_ffn_bq"].astype(f32)
    bkv = I["L0_ffn_bkv"].astype(f32)
    bd = np.zeros((128, 128), BFD)
    for sj in range(8):
        bd[16 * sj:16 * sj + 16, sj::8] = 1
    return {
        "Wg": np.ascontiguousarray(I["L0_ffn_Wg"].astype(f32)),
        "Wq": np.ascontiguousarray(I["L0_ffn_Wq"].astype(f32)),
        "Wkv": np.ascontiguousarray(I["L0_ffn_Wkv"].astype(f32)),
        "bqT": np.ascontiguousarray(bq.T),
        "bkT": np.ascontiguousarray(bkv[:, :D].T),
        "bvT": np.ascontiguousarray(bkv[:, D:].T),
        "bdmask": bd,
        "idf": np.eye(128, dtype=f32),
        "idb": np.eye(128).astype(BFD),
    }


def _stage_pb_consts(I):
    f32 = np.float32
    return {
        "WoF": np.ascontiguousarray(I["L0_ffn_Wo"].astype(f32)),
        "bo": np.ascontiguousarray(I["L0_ffn_bo"].astype(f32).reshape(1, C)),
        "lng": np.ascontiguousarray(I["L0_ffn_ln_g"].astype(f32).reshape(1, C)),
        "lnb2": np.ascontiguousarray(I["L0_ffn_ln_b"].astype(f32).reshape(1, C)),
        "phsel": np.kron(np.eye(16, dtype=f32), np.ones((1, 64), f32)),
    }


def _run_pa(xfull, cA):
    nc = _get("pa", build_pa)
    maps = []
    for core in range(NCORES):
        xs = np.ascontiguousarray(xfull[:, core * SSH:(core + 1) * SSH, :])
        xti = np.ascontiguousarray(xs.transpose(1, 0, 2).reshape(B * SSH, C).T)
        m = {"xs": xs, "xti": xti}
        m.update(cA)
        maps.append(m)
    return run_bass_kernel_spmd(nc, maps, core_ids=list(range(NCORES))).results


def _run_pb(xfull, res_a, cB, emit_z):
    nc = _get("pb_z" if emit_z else "pb", lambda: build_pb(emit_z=emit_z, tile_barrier=True, serial=False))
    zmM = np.ascontiguousarray(np.stack([res_a[c]["zm_o"][0, 0:16] for c in range(NCORES)], 1))
    zmS = np.ascontiguousarray(np.stack([res_a[c]["zm_o"][0, 16:32] for c in range(NCORES)], 1))
    maps = []
    for core in range(NCORES):
        xs = np.ascontiguousarray(xfull[:, core * SSH:(core + 1) * SSH, :])
        m = {"xs": xs, "Gt_i": res_a[core]["Gt_o"], "v1_i": res_a[core]["v1_o"],
             "att_i": res_a[core]["att_o"], "st_i": res_a[core]["st_o"],
             "zmM": zmM, "zmS": zmS}
        m.update(cB)
        maps.append(m)
    return run_bass_kernel_spmd(nc, maps, core_ids=list(range(NCORES))).results


def _run_pc(x1full, z1full_bf, cC):
    nc = _get("pc", lambda: build_pc(serial=False, blk_barrier=False))
    maps = []
    for core in range(NCORES):
        b0 = core * BSH
        x1b = np.ascontiguousarray(x1full[b0:b0 + BSH])
        z1t = np.zeros((BSH, C, TFULL + 1), np.float32)
        z1t[:, :, 1:] = z1full_bf[b0:b0 + BSH].transpose(0, 2, 1).astype(np.float32)
        m = {"x1b": x1b, "z1tb": z1t}
        m.update(cC)
        maps.append(m)
    return run_bass_kernel_spmd(nc, maps, core_ids=list(range(NCORES))).results


def kernel(**inputs):
    I = {k: np.asarray(v) for k, v in inputs.items()}
    x = np.ascontiguousarray(I["L0_attn_x"].astype(np.float32))
    cA = _stage_pa_consts(I)
    cB = _stage_pb_consts(I)
    cC = stage_consts(I)

    ra1 = _run_pa(x, cA)
    rb1 = _run_pb(x, ra1, cB, emit_z=True)
    x1 = np.concatenate([rb1[c]["x1_o"] for c in range(NCORES)], 1)
    z1 = np.concatenate([rb1[c]["z1_o"] for c in range(NCORES)], 1)
    rc = _run_pc(x1, z1, cC)
    x2 = np.concatenate([rc[c]["x2b"] for c in range(NCORES)], 0)
    ra2 = _run_pa(x2, cA)
    rb2 = _run_pb(x2, ra2, cB, emit_z=False)
    x3 = np.concatenate([rb2[c]["x1_o"] for c in range(NCORES)], 1)
    return x3
